# version 10
# speedup vs baseline: 1.5037x; 1.0085x over previous
import math
import contextlib
import numpy as np
import ml_dtypes
import concourse.bass as bass
import concourse.mybir as mybir
from concourse.bass import IndirectOffsetOnAxis
from concourse.bass_utils import run_bass_kernel_spmd

F32 = mybir.dt.float32
BF16 = mybir.dt.bfloat16
U32 = mybir.dt.uint32
I32 = mybir.dt.int32
AF = mybir.ActivationFunctionType
ALU = mybir.AluOpType
AX = mybir.AxisListType
nbf16 = ml_dtypes.bfloat16

B, L, D = 4, 4096, 2048
HALF, TC, DC = 2048, 16, 16
H, KVH, HD, KSEL = 16, 4, 128, 512
DH, DHS = 8192, 1024
SCALE = float(D // H) ** -0.5
SHIFT = 30.0
EPS5, EPS6 = 1e-5, 1e-6
NSEARCH = 38
ENGS = ("sync", "tensor", "vector", "scalar", "gpsimd")

VB1, VGATEB, VAGGB, VBCB, VD1B, VD2B = 0, 1, 2, 3, 4, 5
VING, VINB, VPNG, VPNB, VOUTG, VOUTB, VB2, VMETA = 6, 7, 8, 9, 10, 11, 12, 13


def _cw():
    tp = 2.0 * math.pi
    c1 = float(np.float32(6.28125))
    c2 = float(np.float32(round((tp - c1) * (1 << 21)) / (1 << 21)))
    c3 = float(np.float32(np.float64(tp) - c1 - c2))
    return c1, c2, c3


CW1, CW2, CW3 = _cw()
NBO2 = False
NBWO = False
STRICT = False


class Prog:
    def __init__(self):
        self.streams = {e: [] for e in ENGS}
        self.dirty = {e: False for e in ENGS}
        self.last_compute = {e: None for e in ENGS}
        self.inc_map = {}
        self.sb = 0
        self.ds = 0
        self.ds2 = 0
        self.ds3 = 0
        self.ds4 = 0
        self.f3 = 0
        self.f4 = 0
        self.cc = 0
        self.SB = self.DS = self.DS2 = self.DS3 = self.DS4 = self.CC = None

    def emit(self, eng, fn, compute=True):
        fn.scope = getattr(self, "scope", None)
        self.streams[eng].append(fn)
        self.dirty[eng] = True
        if compute:
            self.last_compute[eng] = len(self.streams[eng]) - 1

    def barrier(self, fence3=None, fence4=None):
        if STRICT:
            fence3, fence4 = self.ds3, self.ds4
        if fence3 is not None:
            self.f3 = max(self.f3, fence3)
        if fence4 is not None:
            self.f4 = max(self.f4, fence4)
        for e in ENGS:
            if self.dirty[e]:
                li = self.last_compute[e]
                if li is not None and (e, li) not in self.inc_map:
                    self.sb += 1
                    self.inc_map[(e, li)] = 1
                self.dirty[e] = False
                self.last_compute[e] = None
        sbt, dst, dst2 = self.sb, self.ds, self.ds2
        f3t, f4t = self.f3, self.f4
        for e in ENGS:
            def w(E):
                raise AssertionError("wait marker must not be called")
            w.wait_info = (sbt, dst, dst2, f3t, f4t)
            w.scope = getattr(self, "scope", None)
            self.streams[e].append(w)

    def replay(self, block, nc=None):
        s = self.streams
        inc_map = self.inc_map
        sems = [self.SB, self.DS, self.DS2, self.DS3, self.DS4]

        def run(E, name):
            stream = s[name]
            n_st = len(stream)
            last = [0, 0, 0, 0, 0]
            cur_scope = [None]
            ctx = [None]

            def set_scope(sc):
                if sc == cur_scope[0] or nc is None:
                    return
                if ctx[0] is not None:
                    ctx[0].__exit__(None, None, None)
                    ctx[0] = None
                if sc is not None:
                    ctx[0] = nc.named_scope(sc)
                    ctx[0].__enter__()
                cur_scope[0] = sc

            for i, f in enumerate(stream):
                wi = getattr(f, "wait_info", None)
                if wi is not None:
                    if i + 1 < n_st and getattr(stream[i + 1], "wait_info", None):
                        continue
                    set_scope(getattr(f, "scope", None))
                    for k, v in enumerate(wi):
                        if v > last[k]:
                            E.wait_ge(sems[k], v)
                            last[k] = v
                    continue
                set_scope(getattr(f, "scope", None))
                inst = f(E)
                n = inc_map.get((name, i))
                if n:
                    inst.then_inc(self.SB, n)
            set_scope(None)

        @block.sync
        def _(E):
            run(E, "sync")

        @block.tensor
        def _(E):
            run(E, "tensor")

        @block.vector
        def _(E):
            run(E, "vector")

        @block.scalar
        def _(E):
            run(E, "scalar")

        @block.gpsimd
        def _(E):
            run(E, "gpsimd")


class KB:
    def __init__(self, debug=(), phases=("ph0", "ffn1", "attn", "ffn3")):
        self.phases = set(phases)
        self.debug = set(debug)
        self.dbg_names = []
        self.nc = bass.Bass("TRN2", target_bir_lowering=False)
        self.P = Prog()
        self.stack = contextlib.ExitStack()
        self._io()

    def _io(self):
        nc = self.nc
        self.ein = {}
        for name, shape in [
            ("x_own", [HALF, D]), ("x_last", [B, D]),
            ("Wq", [D, D]), ("Wkv", [D, 1024]), ("Wo", [D, D]),
            ("W1", [D, 512]), ("gate_w", [D, D]),
            ("d1s", [D, DHS]), ("d2s", [DHS, D]),
            ("aggs", [D, 256]), ("bcs", [D, 256]),
            ("vecs", [16, D]), ("cst", [128, 512]), ("rows", [1, 8192]),
        ]:
            self.ein[name] = nc.dram_tensor(name, shape, F32, kind="ExternalInput")
        self.out_own = nc.dram_tensor("out_own", [HALF, D], F32, kind="ExternalOutput")
        d = nc.dram_tensor
        self.s_xpre = d("s_xpre", [HALF, D], BF16)
        self.s_x1 = d("s_x1", [HALF, D], BF16)
        self.s_out1 = d("s_out1", [HALF, D], F32)
        self.s_ropetab = d("s_ropetab", [L, 128], F32)
        self.s_idx = d("s_idx", [1, KSEL], U32)
        self.s_impfull = d("s_impfull", [L, 1], F32)
        self.s_wo = d("s_wo", [HALF, D], F32)
        self.cc_th_in = d("cc_th_in", [B, D], F32)
        self.cc_th_out = d("cc_th_out", [8 * B, D], F32)
        self.cc_thr_out = d("cc_thr_out", [B, D], F32)
        self.cc_sm_in = d("cc_sm_in", [1, 2048], F32)
        self.cc_sm_out = d("cc_sm_out", [8, 2048], F32)
        self.cc_ag_in = d("cc_ag_in", [1, 1024], F32)
        self.cc_ag_out = d("cc_ag_out", [8, 1024], F32)
        self.cc_xs_in = d("cc_xs_in", [KSEL, D], BF16)
        self.cc_xs_out = d("cc_xs_out", [8 * KSEL, D], BF16)

    def sbuf(self, name, shape, dt):
        return self.stack.enter_context(self.nc.sbuf_tensor(name, list(shape), dt))

    def psumt(self, name, shape):
        return self.stack.enter_context(self.nc.psum_tensor(name, list(shape), F32))

    def dma(self, out, in_, eng="sync", slow=False, nb=False):
        P = self.P
        if eng == "gpsimd":
            if nb:
                DS = P.DS4
                P.ds4 += 16
                ret = P.ds4
            else:
                DS = P.DS2
                P.ds2 += 16
                ret = P.ds2
        else:
            if nb:
                DS = P.DS3
                P.ds3 += 16
                ret = P.ds3
            else:
                DS = P.DS
                P.ds += 16
                ret = P.ds
        if slow:
            P.emit(eng, lambda E: E.dma_start(
                out=out, in_=in_, allow_slow_non_contiguous=True).then_inc(DS, 16),
                compute=False)
        else:
            P.emit(eng, lambda E: E.dma_start(out=out, in_=in_).then_inc(DS, 16),
                   compute=False)
        if eng == "gpsimd":
            self._sw_tick(nb)
        return ret

    def TT(self, out, a, b, op, eng="vector"):
        self.P.emit(eng, lambda E: E.tensor_tensor(out=out, in0=a, in1=b, op=op))

    def TS(self, out, a, s1, op0, s2=None, op1=None, eng="vector"):
        if op1 is None:
            self.P.emit(eng, lambda E: E.tensor_scalar(
                out=out, in0=a, scalar1=s1, scalar2=None, op0=op0))
        else:
            self.P.emit(eng, lambda E: E.tensor_scalar(
                out=out, in0=a, scalar1=s1, scalar2=s2, op0=op0, op1=op1))

    def STT(self, out, a, s, op0, b, op1):
        self.P.emit("vector", lambda E: E.scalar_tensor_tensor(
            out=out, in0=a, scalar=s, in1=b, op0=op0, op1=op1))

    def ACT(self, out, in_, func, bias=None, scale=None):
        kw = dict(out=out, in_=in_, func=func)
        if bias is not None:
            kw["bias"] = bias
        if scale is not None:
            kw["scale"] = scale
        self.P.emit("scalar", lambda E: E.activation(**kw))

    def MM(self, out, lhsT, rhs, start, stop):
        self.P.emit("tensor", lambda E: E.matmul(out, lhsT, rhs, start=start, stop=stop))

    def TRANS(self, out, in_, ident=None):
        idn = ident if ident is not None else self.IDENTB
        idn = idn[:, :] if hasattr(idn, "ap") and not isinstance(idn, bass.AP) else idn
        self.P.emit("tensor", lambda E, _i=idn: E.transpose(out, in_, _i))

    def COPY(self, out, in_, eng="vector"):
        self.P.emit(eng, lambda E: E.tensor_copy(out=out, in_=in_))

    def RED(self, out, in_, op, eng="vector"):
        self.P.emit(eng, lambda E: E.tensor_reduce(out=out, in_=in_, axis=AX.X, op=op))

    def MEMSET(self, ap, v, eng="vector"):
        self.P.emit(eng, lambda E: E.memset(ap, v))

    def CCALL(self, kind, op, ins, outs):
        P = self.P
        P.cc += 1
        t = P.cc
        CC = P.CC
        P.emit("gpsimd", lambda E: E.collective_compute(
            kind, op, replica_groups=[list(range(8))],
            ins=[ins], outs=[outs]).then_inc(CC, 1), compute=False)
        P.emit("gpsimd", lambda E, _t=t: E.wait_ge(CC, _t), compute=False)
        P.emit("gpsimd", lambda E: E.engine_nop(), compute=True)
        self.BAR()

    def _sw_tick(self, nb=False):
        self._swn = getattr(self, "_swn", 0) + 1
        if self._swn % 2 == 0:
            P = self.P
            if nb:
                tgt = max(P.ds4 - 32, 0)
                if tgt:
                    P.emit("gpsimd", lambda E, _t=tgt: E.wait_ge(P.DS4, _t),
                           compute=False)
            else:
                tgt = P.ds2
                P.emit("gpsimd", lambda E, _t=tgt: E.wait_ge(P.DS2, _t), compute=False)

    def IGATHER(self, out, src, idxcol, bounds, tag=""):
        P = self.P
        DS = P.DS2
        P.ds2 += 16
        def f(E, _o=out, _s=src, _i=idxcol, _b=bounds, _t=tag):
            try:
                return E.indirect_dma_start(
                    out=_o, out_offset=None, in_=_s,
                    in_offset=IndirectOffsetOnAxis(ap=_i, axis=0),
                    bounds_check=_b, oob_is_err=False).then_inc(DS, 16)
            except Exception as ex:
                raise RuntimeError(f"IGATHER[{_t}] failed: {ex}") from ex
        P.emit("gpsimd", f, compute=False)
        self._sw_tick()

    def ISCATTER(self, out, slotcol, vals, bounds):
        P = self.P
        DS = P.DS2
        P.ds2 += 16
        P.emit("gpsimd", lambda E: E.indirect_dma_start(
            out=out, out_offset=IndirectOffsetOnAxis(ap=slotcol, axis=1),
            in_=vals, in_offset=None,
            bounds_check=bounds, oob_is_err=False).then_inc(DS, 16), compute=False)
        self._sw_tick()

    def BAR(self, fence3=None, fence4=None):
        self.P.barrier(fence3=fence3, fence4=fence4)

    def dbg(self, name, ap, shape, dt=F32):
        if name not in self.debug:
            return
        t = self.nc.dram_tensor("dbg_" + name, list(shape), dt, kind="ExternalOutput")
        self.dbg_names.append(name)
        self.dma(t.ap()[:, :], ap)
        self.BAR()

    def build(self):
        with self.stack:
            self.SB = self.stack.enter_context(self.nc.semaphore("SBAR"))
            self.DSm = self.stack.enter_context(self.nc.semaphore("DSEM"))
            self.DS2m = self.stack.enter_context(self.nc.semaphore("DSWSEM"))
            self.DS3m = self.stack.enter_context(self.nc.semaphore("DNB"))
            self.DS4m = self.stack.enter_context(self.nc.semaphore("DNBSW"))
            self.CCm = self.stack.enter_context(self.nc.semaphore("CCSEM"))
            self.P.SB, self.P.DS, self.P.CC = self.SB, self.DSm, self.CCm
            self.P.DS2 = self.DS2m
            self.P.DS3 = self.DS3m
            self.P.DS4 = self.DS4m
            self._alloc()
            for i, v in enumerate([EPS5, EPS6, SCALE, -SHIFT]):
                self.nc.const_aps.register(F32, float(v), self.CONSTC[:, i:i + 1])
            self.P.scope = "ph0"
            self._ph0()
            if "ffn1" in self.phases:
                self.P.scope = "ffn1"
                self._ffn(call=1)
            if "attn" in self.phases:
                self._attn()
            if "ffn3" in self.phases:
                self.P.scope = "ffn3"
                self._ffn(call=3)
            else:
                self.MEMSET(self.SCR[:, 0:D], 0.0)
                self.BAR()
                for tc in range(TC):
                    self.dma(self.out_own.ap()[tc * 128:(tc + 1) * 128, :], self.SCR[:, 0:D])
                self.BAR()
            self.P.barrier(fence3=self.P.ds3, fence4=self.P.ds4)
            with self.nc.Block() as block:
                self.P.replay(block, self.nc)
        return self.nc

    def _alloc(self):
        sb = self.sbuf
        self.IDENT = sb("IDENT", [128, 128], F32)
        self.IDENTB = sb("IDENTB", [128, 128], BF16)
        self.UT = sb("UT", [128, 128], F32)
        self.SUT32 = sb("SUT32", [32, 32], F32)
        self.POSALL = sb("POSALL", [128, 32], F32)
        self.POSU = sb("POSU", [128, 32], U32)
        self.COSFB = sb("COSFB", [128, HALF], BF16)
        self.SINFB = sb("SINFB", [128, HALF], BF16)
        self.METAB = sb("METAB", [128, 8], F32)
        self.STAT = sb("STAT", [128, 64], F32)
        self.SMALL = sb("SMALL", [128, 512], F32)
        self.SMALLB = sb("SMALLB", [128, 512], BF16)
        self.IMPS = sb("IMPS", [128, 40], F32)
        self.IDXU = sb("IDXU", [128, 16], U32)
        self.PVAL = sb("PVAL", [128, 16], F32)
        self.CONSTC = sb("CONSTC", [128, 8], F32)
        self.ONESB = sb("ONESB", [1, 128], BF16)
        self.ONESF = sb("ONESF", [1, 128], F32)
        self.HALFF = sb("HALFF", [1, 128], F32)
        self.XH = sb("XH", [128, D], BF16)
        self.INVB = sb("INVB", [128, 64], F32)
        self.XT = sb("XT", [128, TC * D], BF16)
        self.XT2 = sb("XT2", [128, TC * D], BF16)
        self.SCR = sb("SCR", [128, D], F32)
        self.WS = sb("WS", [128, 2 * D], BF16)
        self.WSF = sb("WSF", [128, D], F32)
        self.POOLA = sb("POOLA", [128, 2 * D], F32)
        self.POOLB = sb("POOLB", [128, 2 * D], F32)
        self.HB = self.POOLA.bitcast(BF16)
        self.QP = self.POOLA[:, 0:2048]
        self.KEEP = self.POOLB.bitcast(BF16)
        self.GAM = self.POOLA.bitcast(BF16)
        self.IINT = self.POOLB.bitcast(I32)
        self.PS_A = self.psumt("PS_A", [128, 2048])
        self.PS_B = self.psumt("PS_B", [128, 1024])
        self.PS_T = self.psumt("PS_T", [128, 1024], BF16) if False else None
        self.PS_T = self.stack.enter_context(self.nc.psum_tensor("PS_T", [128, 1024], BF16))
        self.PS_TF = self.psumt("PS_TF", [128, 512])

    def _ph0(self):
        e = self.ein
        stop = getattr(self, "ph0_stop", 99)
        for i, v in enumerate([EPS5, EPS6, SCALE, -SHIFT]):
            self.MEMSET(self.CONSTC[:, i:i + 1], float(v))
        self.dma(self.IDENT[:, :], e["cst"].ap()[:, 0:128])
        self.dma(self.UT[:, :], e["cst"].ap()[:, 128:256])
        self.dma(self.POSALL[:, :], e["cst"].ap()[:, 256:288])
        self.dma(self.SUT32[:, :], e["cst"].ap()[0:32, 320:352])
        self.BAR()
        self.COPY(self.IDENTB[:, :], self.IDENT[:, :])
        self.COPY(self.POSU[:, :], self.POSALL[:, :])
        self.MEMSET(self.ONESF[0:1, :], 1.0)
        self.MEMSET(self.HALFF[0:1, :], 0.5)
        self.MEMSET(self.ONESB[0:1, :], 1.0)
        self.BAR()
        self.dma(self.SCR[0:1, 0:8], e["rows"].ap()[0:1, 6208:6216])
        self.BAR()
        self.MM(self.PS_B[:, 0:8], self.ONESF[0:1, :], self.SCR[0:1, 0:8], True, True)
        self.BAR()
        self.COPY(self.METAB[:, :], self.PS_B[:, 0:8])
        if stop <= 1:
            return
        self.dma(self.SMALL[0:1, 448:512], e["rows"].ap()[0:1, 6144:6208])
        self.dma(self.SCR[0:1, 0:D], e["rows"].ap()[0:1, 0:D])
        self.BAR()
        inv = self.SMALL[0:1, 448:512]
        if stop <= 2:
            return
        for s in range(getattr(self, "fm_iters", 4)):
            self.MM(self.PS_B[0:64, 0:512], inv, self.SCR[0:1, s * 512:(s + 1) * 512],
                    True, True)
            self.BAR()
            self._sincos(self.PS_B[0:64, 0:512], 64, 512,
                         self.SINFB[64:128, s * 512:(s + 1) * 512],
                         self.COSFB[0:64, s * 512:(s + 1) * 512])
            self.BAR()
        if stop <= 3:
            return
        if getattr(self, "fm_dup", True) is False:
            return
        self.dma(self.COSFB[64:128, :], self.COSFB[0:64, :])
        self.dma(self.SINFB[0:64, :], self.SINFB[64:128, :])
        self.BAR()
        self.TS(self.SINFB[0:64, :], self.SINFB[0:64, :], -1.0, ALU.mult)
        self.BAR()
        if getattr(self, "fm_invb", True):
            self.MM(self.PS_B[:, 0:64], self.ONESF[0:1, :], inv, True, True)
            self.BAR()
            self.COPY(self.INVB[:, :], self.PS_B[:, 0:64])
            self.BAR()

    def _sincos(self, ang_psum, prows, n, out_sin, out_cos, scr=None):
        stop = getattr(self, "sc_stop", 99)
        scr = scr if scr is not None else self.POOLB
        a = scr[0:prows, 0:n]
        k = scr[0:prows, n:2 * n]
        ii = scr.bitcast(I32)[0:prows, 2 * n:3 * n]
        self.COPY(a, ang_psum)
        self.BAR()
        if stop <= 1:
            self.COPY(out_sin, a)
            self.COPY(out_cos, a)
            self.BAR()
            return
        self.TS(k, a, 1.0 / (2 * math.pi), ALU.mult)
        if stop <= 2:
            self.COPY(out_sin, k)
            self.COPY(out_cos, k)
            self.BAR()
            return
        self.COPY(ii, k)
        self.COPY(k, ii)
        if stop <= 3:
            self.COPY(out_sin, k)
            self.COPY(out_cos, k)
            self.BAR()
            return
        self.STT(a, k, -CW1, ALU.mult, a, ALU.add)
        self.STT(a, k, -CW2, ALU.mult, a, ALU.add)
        self.STT(a, k, -CW3, ALU.mult, a, ALU.add)
        self.TS(k, a, math.pi, ALU.is_ge)
        self.STT(k, k, -2 * math.pi, ALU.mult, a, ALU.add)
        self.BAR()
        if stop <= 4:
            self.COPY(out_sin, k)
            self.COPY(out_cos, k)
            self.BAR()
            return
        self.ACT(out_sin, k, AF.Sin)
        self.BAR()
        self.TS(a, a, math.pi / 2, ALU.add)
        self.TS(k, a, math.pi, ALU.is_ge)
        self.STT(k, k, -2 * math.pi, ALU.mult, a, ALU.add)
        self.BAR()
        self.ACT(out_cos, k, AF.Sin)
        self.BAR()

    def _ln_stats(self, src, prows, eps, r_col, nmr_col, base=0):
        S = self.STAT
        mean = S[0:prows, base + 24:base + 25]
        ex2 = S[0:prows, base + 25:base + 26]
        var = S[0:prows, base + 26:base + 27]
        mm2 = S[0:prows, base + 36:base + 37]
        sq = S[0:prows, base + 37:base + 38]
        self.RED(mean, src, ALU.add)
        self.ACT(self.PS_A[0:prows, 0:2048], src, AF.Square)
        self.BAR()
        self.TS(mean, mean, 1.0 / 2048.0, ALU.mult)
        self.RED(ex2, self.PS_A[0:prows, 0:2048], ALU.add)
        self.BAR()
        self.TS(ex2, ex2, 1.0 / 2048.0, ALU.mult)
        self.TT(mm2, mean, mean, ALU.mult)
        self.BAR()
        self.TT(var, ex2, mm2, ALU.subtract)
        self.BAR()
        self.ACT(sq, var, AF.Sqrt, bias=float(eps))
        self.BAR()
        self.P.emit("vector", lambda E, _r=r_col, _s=sq: E.reciprocal(out=_r, in_=_s))
        self.BAR()
        self.ACT(nmr_col, mean, AF.Identity, scale=r_col)
        self.BAR()
        self.TS(nmr_col, nmr_col, -1.0, ALU.mult)
        self.BAR()

    def _ffn(self, call):
        e = self.ein
        src_rows = e["x_own"].ap() if call == 1 else self.s_out1.ap()
        XPT = self.XT2 if call == 3 else self.XT
        G = self.XT if call == 3 else self.XT2
        r_c = self.STAT[:, 40:41]
        nm_c = self.STAT[:, 41:42]

        for j, vec in ((0, VOUTG), (1, VOUTB)):
            self.dma(self.SCR[0:1, 0:D], e["vecs"].ap()[vec:vec + 1, :])
            self.BAR()
            for s in range(4):
                self.MM(self.PS_A[:, s * 512:(s + 1) * 512],
                        self.HALFF[0:1, :],
                        self.SCR[0:1, s * 512:(s + 1) * 512], True, True)
            self.BAR()
            self.COPY(self.GAM[:, j * D:(j + 1) * D], self.PS_A[:, 0:D])
            self.BAR()

        self.P.scope = f"f{call}A"
        for tc in range(TC):
            xt = self.SCR[:, 0:D]
            self.dma(xt, src_rows[tc * 128:(tc + 1) * 128, :])
            self.BAR()
            self._ln_stats(xt, 128, EPS5, r_c, nm_c)
            self.BAR()
            xpb = self.XH[:, 0:D]
            self.ACT(xpb, xt, AF.Identity, bias=nm_c, scale=r_c)
            self.BAR()
            self.dma(self.s_xpre.ap()[tc * 128:(tc + 1) * 128, :], xpb)
            xpt3 = XPT[:, 0:TC * D].rearrange("p (dc n) -> p dc n", dc=16)
            for g in range(2):
                for j in range(8):
                    dc = g * 8 + j
                    self.TRANS(self.PS_T[:, j * 128:(j + 1) * 128],
                               xpb[:, dc * 128:(dc + 1) * 128])
                self.BAR()
                self.COPY(xpt3[:, g * 8:(g + 1) * 8, tc * 128:(tc + 1) * 128],
                          self.PS_T[:, 0:1024].rearrange("p (j n) -> p j n", j=8))
                self.BAR()

        if getattr(self, "ffn_stop", 99) <= 1:
            return
        self.dbg(f"xpt{call}", XPT[:, 0:TC * D], [128, TC * D], BF16)
        self.P.scope = f"f{call}th"
        if call == 1:
            self.dma(self.POOLB[0:B, D:2 * D], e["x_last"].ap()[:, :])
            self.BAR()
        yl = self.POOLB[0:B, D:2 * D]
        yln = self.POOLB[0:B, 0:D]
        rl = self.STAT[0:B, 44:45]
        nl = self.STAT[0:B, 45:46]
        self._ln_stats(yl, B, EPS5, rl, nl, base=0)
        self.BAR()
        self.TS(yln, yl, rl, ALU.mult, nl, ALU.add)
        self.BAR()
        self._thought(G)
        self.dbg(f"bcb{call}", self.GAM[:, 2 * D:3 * D], [128, D], BF16)

        if getattr(self, "ffn_stop", 99) <= 2:
            return
        self.P.scope = f"f{call}gate"
        WSFB = self.WSF.bitcast(BF16)

        gtok = {}

        def ldgate(slab):
            buf = self.WS if slab % 2 == 0 else WSFB
            self.dma(buf[:, 0:4096].rearrange("p (c n) -> p c n", c=16),
                     e["gate_w"].ap()[:, slab * 256:(slab + 1) * 256]
                     .rearrange("(c p) n -> p c n", p=128), eng="gpsimd", nb=True)
            gtok[slab] = self.dma(
                self.SMALLB[0:1, (slab % 2) * 256:(slab % 2) * 256 + 256],
                e["vecs"].ap()[VGATEB:VGATEB + 1, slab * 256:(slab + 1) * 256],
                eng="gpsimd", nb=True)
        ldgate(0)
        self.BAR(fence4=gtok[0])
        for slab in range(8):
            buf = self.WS if slab % 2 == 0 else WSFB
            bcol = (slab % 2) * 256
            for tc in range(TC):
                ps = self.PS_A[:, (tc % 8) * 256:(tc % 8 + 1) * 256]
                for dc in range(DC):
                    self.MM(ps, XPT[:, dc * D + tc * 128:dc * D + (tc + 1) * 128],
                            buf[:, dc * 256:(dc + 1) * 256], dc == 0, False)
                self.MM(ps, self.ONESB[0:1, :], self.SMALLB[0:1, bcol:bcol + 256],
                        False, True)
                if tc % 8 == 7:
                    self.BAR()
                    for t2 in range(tc - 7, tc + 1):
                        self.ACT(G[:, t2 * D + slab * 256:t2 * D + (slab + 1) * 256],
                                 self.PS_A[:, (t2 % 8) * 256:(t2 % 8 + 1) * 256],
                                 AF.Sigmoid)
                    if tc == 15:
                        self.BAR(fence4=gtok.get(slab + 1))
                    else:
                        self.BAR()
                        if slab + 1 < 8:
                            ldgate(slab + 1)
        self.dbg(f"g{call}", G[:, 0:D], [128, D], BF16)

        if getattr(self, "ffn_stop", 99) <= 3:
            return
        self.P.scope = f"f{call}C"
        for tc in range(TC):
            xpb = self.XH[:, 0:D]
            self.dma(xpb, self.s_xpre.ap()[tc * 128:(tc + 1) * 128, :])
            xt = self.SCR[:, 0:D]
            self.dma(xt, src_rows[tc * 128:(tc + 1) * 128, :])
            self.BAR()
            u = self.WS[:, 0:D]
            self.TT(u, G[:, tc * D:(tc + 1) * D], self.GAM[:, 2 * D:3 * D], ALU.mult)
            self.BAR()
            self.TT(u, u, xpb, ALU.add)
            self.BAR()
            self._ln_stats(u, 128, EPS6, r_c, nm_c)
            self.BAR()
            xh = self.WS[:, D:2 * D]
            self.ACT(xh, u, AF.Identity, bias=nm_c, scale=r_c)
            self.BAR()
            t1 = self.XH[:, 0:D]
            self.TT(t1, xh, self.GAM[:, 0:D], ALU.mult)
            self.BAR()
            f2 = self.WS[:, 0:D]
            self.TT(f2, t1, self.GAM[:, D:2 * D], ALU.add)
            self.BAR()
            out = self.WSF[:, 0:D]
            self.TT(out, f2, xt, ALU.add)
            self.BAR()
            if call == 1:
                self.dma(self.s_out1.ap()[tc * 128:(tc + 1) * 128, :], out)
                self._ln_stats(out, 128, EPS5, r_c, nm_c)
                self.BAR()
                x1b = self.XH[:, 0:D]
                self.ACT(x1b, out, AF.Identity, bias=nm_c, scale=r_c)
                self.BAR()
                self.dma(self.s_x1.ap()[tc * 128:(tc + 1) * 128, :], x1b)
                x1t3 = self.XT[:, 0:TC * D].rearrange("p (dc n) -> p dc n", dc=16)
                for g in range(2):
                    for j in range(8):
                        dc = g * 8 + j
                        self.TRANS(self.PS_T[:, j * 128:(j + 1) * 128],
                                   x1b[:, dc * 128:(dc + 1) * 128])
                    self.BAR()
                    self.COPY(x1t3[:, g * 8:(g + 1) * 8, tc * 128:(tc + 1) * 128],
                              self.PS_T[:, 0:1024].rearrange("p (j n) -> p j n", j=8))
                    self.BAR()
            else:
                self.dma(self.out_own.ap()[tc * 128:(tc + 1) * 128, :], out)
        self.BAR()

    def _thought(self, G):
        e = self.ein
        yn = self.POOLB[0:B, 0:D]
        th = self.POOLB[0:B, D:2 * D]
        tn = self.POOLB[0:B, 0:D]
        vrow = self.WSF[0:B, 0:D]
        rl = self.STAT[0:B, 44:45]
        nl = self.STAT[0:B, 45:46]

        for dc in range(DC):
            self.dma(G[:, dc * 1024:(dc + 1) * 1024],
                     e["d1s"].ap()[dc * 128:(dc + 1) * 128, :], eng="gpsimd", nb=True)
        for kc in range(8):
            self.d12tok = self.dma(
                G[:, 16384 + kc * 2048:16384 + (kc + 1) * 2048],
                e["d2s"].ap()[kc * 128:(kc + 1) * 128, :], eng="gpsimd", nb=True)

        def vload(vec, n=D):
            for p in range(B):
                self.dma(vrow[p:p + 1, 0:n], e["vecs"].ap()[vec:vec + 1, 0:n])

        def ln_gb(src, dst, gvec, bvec):
            self._ln_stats(src, B, EPS6, rl, nl, base=0)
            self.BAR()
            self.TS(dst, src, rl, ALU.mult, nl, ALU.add)
            self.BAR()
            vload(gvec)
            self.BAR()
            self.TT(dst, dst, vrow, ALU.mult)
            self.BAR()
            vload(bvec)
            self.BAR()
            self.TT(dst, dst, vrow, ALU.add)
            self.BAR()

        def trows(src, width):
            nt = width // 128
            for j in range(nt):
                self.TRANS(self.PS_TF[:, j * 4:(j + 1) * 4],
                           src[:, j * 128:(j + 1) * 128], self.IDENT[0:B, 0:B])
            self.BAR()
            self.COPY(self.SMALLB[:, 0:nt * 4], self.PS_TF[:, 0:nt * 4])
            self.BAR()

        ln_gb(self.POOLB[0:B, 0:D], yn, VING, VINB)


        trows(yn, D)
        for g4 in range(4):
            self.dma(self.WS[:, g4 * 1024:g4 * 1024 + 1024]
                     .rearrange("p (c n) -> p c n", c=4),
                     e["aggs"].ap()[g4 * 512:(g4 + 1) * 512, :]
                     .rearrange("(c p) n -> p c n", p=128), eng="gpsimd")
        self.BAR()
        for dc in range(DC):
            self.MM(self.PS_A[0:B, 0:256], self.SMALLB[:, dc * 4:(dc + 1) * 4],
                    self.WS[:, dc * 256:(dc + 1) * 256], dc == 0, dc == DC - 1)
        self.BAR()
        self.COPY(self.SCR[0:B, 0:256], self.PS_A[0:B, 0:256])
        self.BAR()
        self.dma(self.cc_ag_in.ap()[0:1, :], self.SCR[0:B, 0:256])
        self.BAR()
        self.CCALL("AllGather", ALU.bypass, self.cc_ag_in.ap()[:, :].opt(),
                   self.cc_ag_out.ap()[:, :].opt())
        for r in range(8):
            self.dma(th[:, r * 256:(r + 1) * 256], self.cc_ag_out.ap()[r:r + 1, :])
        self.BAR()
        vload(VAGGB)
        self.BAR()
        self.TT(th, th, vrow, ALU.add)
        self.BAR()
        self.dbg("thinit", th, [B, D])

        self.BAR(fence4=self.d12tok)
        for step in range(4):
            ln_gb(th, tn, VPNG, VPNB)
            trows(tn, D)
            for dc in range(DC):
                for hh in range(2):
                    self.MM(self.PS_A[0:B, hh * 512:(hh + 1) * 512],
                            self.SMALLB[:, dc * 4:(dc + 1) * 4],
                            G[:, dc * 1024 + hh * 512:dc * 1024 + (hh + 1) * 512],
                            dc == 0, dc == DC - 1)
            self.BAR()
            hsl = self.POOLB[0:B, 0:DHS]
            vload(VD1B, DHS)
            self.BAR()
            self.TT(hsl, self.PS_A[0:B, 0:DHS], vrow[:, 0:DHS], ALU.add)
            self.BAR()
            self.ACT(hsl, hsl, AF.Gelu)
            self.BAR()
            trows(hsl, DHS)
            for kc in range(8):
                for qq in range(4):
                    self.MM(self.PS_A[0:B, qq * 512:(qq + 1) * 512],
                            self.SMALLB[:, kc * 4:(kc + 1) * 4],
                            G[:, 16384 + kc * 2048 + qq * 512:
                               16384 + kc * 2048 + (qq + 1) * 512],
                            kc == 0, kc == 7)
            self.BAR()
            self.COPY(self.SCR[0:B, 0:D], self.PS_A[0:B, 0:D])
            self.BAR()
            self.dma(self.cc_th_in.ap()[:, :], self.SCR[0:B, 0:D])
            self.BAR()
            self.CCALL("AllReduce", ALU.add, self.cc_th_in.ap()[:, :].opt(),
                       self.cc_thr_out.ap()[:, :].opt())
            self.dma(self.SCR[0:B, 0:D], self.cc_thr_out.ap()[:, :])
            self.BAR()
            vload(VD2B)
            self.BAR()
            self.TT(self.SCR[0:B, 0:D], self.SCR[0:B, 0:D], vrow, ALU.add)
            self.BAR()
            self.TT(th, th, self.SCR[0:B, 0:D], ALU.add)
            self.BAR()
            self.dbg(f"thstep{step}", th, [B, D])

        trows(th, D)
        for g4 in range(4):
            self.dma(self.WS[:, g4 * 1024:g4 * 1024 + 1024]
                     .rearrange("p (c n) -> p c n", c=4),
                     e["bcs"].ap()[g4 * 512:(g4 + 1) * 512, :]
                     .rearrange("(c p) n -> p c n", p=128), eng="gpsimd")
        self.BAR()
        for dc in range(DC):
            self.MM(self.PS_A[0:B, 0:256], self.SMALLB[:, dc * 4:(dc + 1) * 4],
                    self.WS[:, dc * 256:(dc + 1) * 256], dc == 0, dc == DC - 1)
        self.BAR()
        self.COPY(self.SCR[0:B, 0:256], self.PS_A[0:B, 0:256])
        self.BAR()
        self.dma(self.cc_ag_in.ap()[0:1, :], self.SCR[0:B, 0:256])
        self.BAR()
        self.CCALL("AllGather", ALU.bypass, self.cc_ag_in.ap()[:, :].opt(),
                   self.cc_ag_out.ap()[:, :].opt())
        for r in range(8):
            self.dma(tn[:, r * 256:(r + 1) * 256], self.cc_ag_out.ap()[r:r + 1, :])
        self.BAR()
        vload(VBCB)
        self.BAR()
        self.TT(tn, tn, vrow, ALU.add)
        self.BAR()
        self.dma(self.cc_th_in.ap()[:, :], tn)
        self.BAR()
        self.COPY(self.IDXU[0:4, 8:9], self.METAB[0:4, 4:5])
        self.BAR()
        self.IGATHER(self.SCR[0:4, 0:D], self.cc_th_in.ap()[:, :],
                     self.IDXU[0:4, 8:9], B - 1, tag="bc")
        self.BAR()
        for s in range(4):
            self.MM(self.PS_A[:, s * 512:(s + 1) * 512], self.ONESF[0:1, :],
                    self.SCR[0:1, s * 512:(s + 1) * 512], True, True)
        self.BAR()
        self.COPY(self.GAM[:, 2 * D:3 * D], self.PS_A[:, 0:D])
        self.BAR()

    def _attn(self):
        e = self.ein
        self.P.scope = "a_imp"
        W1B = self.XT2
        for dc in range(DC):
            self.dma(W1B[:, dc * 512:(dc + 1) * 512],
                     e["W1"].ap()[dc * 128:(dc + 1) * 128, :], eng="gpsimd")
        self.BAR()
        self.dma(self.PVAL[:, 0:4],
                 e["vecs"].ap()[VB1:VB1 + 1, 0:512].rearrange("one (m p) -> (one p) m", p=128), slow=True)
        self.BAR()
        for tcg in range(4):
            for m in range(4):
                ps = self.PS_A[:, m * 512:(m + 1) * 512]
                for dc in range(DC):
                    self.MM(ps, W1B[:, dc * 512 + m * 128:dc * 512 + (m + 1) * 128],
                            self.XT[:, dc * D + tcg * 512:dc * D + (tcg + 1) * 512],
                            dc == 0, dc == DC - 1)
            self.BAR()
            for m in range(4):
                self.ACT(self.HB[:, m * HALF + tcg * 512:m * HALF + (tcg + 1) * 512],
                         self.PS_A[:, m * 512:(m + 1) * 512], AF.Gelu,
                         bias=self.PVAL[:, m:m + 1])
            self.BAR()
        self.dma(self.PVAL[:, 4:8],
                 e["rows"].ap()[0:1, 6216:6216 + 512].rearrange("one (m p) -> (one p) m", p=128), slow=True)
        self.BAR()
        self.COPY(self.SMALLB[:, 0:4], self.PVAL[:, 4:8])
        self.BAR()
        for tcg in range(4):
            for m in range(4):
                self.MM(self.PS_B[0:1, 0:512], self.SMALLB[:, m:m + 1],
                        self.HB[:, m * HALF + tcg * 512:m * HALF + (tcg + 1) * 512],
                        m == 0, m == 3)
            self.BAR()
            self.COPY(self.SCR[0:1, D - 2048 + tcg * 512:D - 2048 + (tcg + 1) * 512], self.PS_B[0:1, 0:512])
            self.BAR()
        self.TS(self.SCR[0:1, 0:D], self.SCR[0:1, 0:D], self.METAB[0:1, 5:6], ALU.add)
        self.BAR()
        self.dbg("improw", self.SCR[0:1, 0:D], [1, HALF])

        if getattr(self, "attn_stop", 99) <= 1:
            return
        self.P.scope = "a_exch"
        self.dma(self.cc_sm_in.ap()[0:1, 0:HALF], self.SCR[0:1, 0:D])
        self.BAR()
        self.CCALL("AllGather", ALU.bypass, self.cc_sm_in.ap()[:, :].opt(),
                   self.cc_sm_out.ap()[:, :].opt())
        self.TT(self.PVAL[:, 15:16], self.POSALL[:, 0:1], self.METAB[:, 6:7], ALU.add)
        self.BAR()
        self.COPY(self.IDXU[0:2, 8:9], self.PVAL[0:2, 15:16])
        self.BAR()
        self.IGATHER(self.SCR[0:2, 0:HALF], self.cc_sm_out.ap()[:, :],
                     self.IDXU[0:2, 8:9], 7, tag="imp2")
        self.BAR()
        for hh in range(2):
            self.dma(self.s_impfull.ap()[hh * HALF:(hh + 1) * HALF, :],
                     self.SCR[hh:hh + 1, 0:HALF])
        self.BAR()
        self.dma(self.IMPS[:, 0:32],
                 self.s_impfull.ap()[:, 0:1].rearrange("(c p) one -> p (c one)", p=128), slow=True)
        self.BAR()
        self.dbg("impfull", self.IMPS[:, 0:32], [128, 32])

        self.P.scope = "a_qp"
        self.dma(self.SCR[0:1, 0:D], e["rows"].ap()[0:1, 0:D])
        self.BAR()
        for s in range(4):
            self.MM(self.PS_A[:, s * 512:(s + 1) * 512], self.ONESF[0:1, :],
                    self.SCR[0:1, s * 512:(s + 1) * 512], True, True)
        self.BAR()
        self.COPY(self.QP[:, :], self.PS_A[:, 0:D])
        self.BAR()

        if getattr(self, "attn_stop", 99) <= 2:
            return
        self.P.scope = "a_search"
        S = self.SMALL
        lo, hi, mid, cnt, pred, t0 = (S[0:1, 10:11], S[0:1, 11:12], S[0:1, 12:13],
                                      S[0:1, 13:14], S[0:1, 14:15], S[0:1, 15:16])
        midb = S[:, 16:17]
        self.RED(S[:, 18:19], self.IMPS[:, 0:32], ALU.max)
        rowmin = S[:, 19:20]
        self.P.emit("vector", lambda E: E.tensor_reduce(
            out=rowmin, in_=self.IMPS[:, 0:32], axis=AX.X, op=ALU.min))
        self.BAR()
        self.TRANS(self.PS_TF[0:1, 0:128], S[:, 18:19], self.IDENT)
        self.TRANS(self.PS_TF[0:1, 128:256], rowmin, self.IDENT)
        self.BAR()
        self.RED(hi, self.PS_TF[0:1, 0:128], ALU.max)
        self.RED(lo, self.PS_TF[0:1, 128:256], ALU.min)
        self.BAR()
        self.TS(lo, lo, -0.001, ALU.add)
        self.BAR()
        onecol128 = S[:, 21:22]
        self.MEMSET(onecol128, 1.0)
        self.BAR()
        for it in range(NSEARCH):
            self.TT(S[0:1, 17:18], lo, hi, ALU.add)
            self.BAR()
            self.MM(self.PS_B[:, 256:257], self.HALFF[0:1, :], S[0:1, 17:18],
                    True, True)
            self.BAR()
            cmp_ = S[:, 22:54]
            rc = S[:, 54:55]
            self.P.emit("vector", lambda E, _c=cmp_, _r=rc: E.tensor_scalar(
                out=_c, in0=self.IMPS[:, 0:32], scalar1=self.PS_B[:, 256:257],
                scalar2=0.0, op0=ALU.is_gt, op1=ALU.add, accum_out=_r))
            self.COPY(mid, self.PS_B[0:1, 256:257])
            self.BAR()
            self.MM(self.PS_B[0:1, 257:258], onecol128, rc, True, True)
            self.BAR()
            self.TS(pred, self.PS_B[0:1, 257:258], 511.5, ALU.is_gt)
            self.BAR()
            self.TT(t0, mid, lo, ALU.subtract)
            self.TT(S[0:1, 18:19], hi, mid, ALU.subtract)
            self.BAR()
            self.TT(t0, t0, pred, ALU.mult)
            self.TT(S[0:1, 18:19], S[0:1, 18:19], pred, ALU.mult)
            self.BAR()
            self.TT(lo, lo, t0, ALU.add)
            self.TT(hi, mid, S[0:1, 18:19], ALU.add)
            self.BAR()
        self.MM(self.PS_B[:, 256:257], self.ONESF[0:1, :], lo, True, True)
        self.BAR()
        lob = S[:, 16:17]
        self.COPY(lob, self.PS_B[:, 256:257])
        self.BAR()
        gt = S[:, 22:54]
        self.TS(gt, self.IMPS[:, 0:32], lob, ALU.is_gt)
        self.BAR()
        m1 = self.SCR[:, 0:32]
        m2 = self.SCR[:, 160:192]
        self.TS(m2, gt, -1.0e9, ALU.mult, 1.0e9, ALU.add)
        self.TT(m1, self.IMPS[:, 0:32], gt, ALU.mult)
        self.BAR()
        self.TT(m1, m1, m2, ALU.add)
        rmin = S[:, 54:55]
        self.BAR()
        self.P.emit("vector", lambda E: E.tensor_reduce(
            out=rmin, in_=m1, axis=AX.X, op=ALU.min))
        self.BAR()
        self.COPY(self.STAT[:, 60:61], rmin)
        self.COPY(self.STAT[:, 62:63], lob)
        self.TRANS(self.PS_TF[0:1, 0:128], rmin, self.IDENT)
        self.BAR()
        self.COPY(self.STAT[0:1, 61:62], self.PS_TF[0:1, 0:1])
        tau = S[0:1, 10:11]
        self.P.emit("vector", lambda E: E.tensor_reduce(
            out=tau, in_=self.PS_TF[0:1, 0:128], axis=AX.X, op=ALU.min))
        self.BAR()
        self.COPY(self.STAT[0:1, 63:64], tau)
        self.BAR()
        self.dbg("snap", self.STAT[:, 56:64], [128, 8])
        taub = S[:, 16:17]
        self.MM(self.PS_B[:, 256:257], self.ONESF[0:1, :], tau, True, True)
        self.BAR()
        self.COPY(taub, self.PS_B[:, 256:257])
        self.BAR()
        gt01 = self.SCR[:, 32:64]
        self.TS(gt01, self.IMPS[:, 0:32], taub, ALU.is_gt)
        eq01 = self.SCR[:, 64:96]
        self.TS(eq01, self.IMPS[:, 0:32], taub, ALU.is_equal)
        rc = S[:, 54:55]
        self.BAR()
        self.RED(rc, gt01, ALU.add)
        self.BAR()
        self.MM(self.PS_B[0:1, 257:258], onecol128, rc, True, True)
        self.BAR()
        ngt = S[0:1, 11:12]
        self.COPY(ngt, self.PS_B[0:1, 257:258])
        self.BAR()
        need = S[0:1, 12:13]
        self.TS(need, ngt, -512.0, ALU.add, -1.0, ALU.mult)
        self.BAR()
        needb = S[:, 17:18]
        self.MM(self.PS_B[:, 256:257], self.ONESF[0:1, :], need, True, True)
        self.BAR()
        self.COPY(needb, self.PS_B[:, 256:257])
        self.BAR()
        if getattr(self, "attn_stop", 99) <= 22:
            return
        sel = self.SCR[:, 96:128]
        self._cumsum_inc(eq01, self.SCR[:, 128:160])
        self.BAR()
        le = self.SCR[:, 160:192]
        self.TS(le, self.SCR[:, 128:160], needb, ALU.is_le)
        self.BAR()
        self.TT(le, le, eq01, ALU.mult)
        self.BAR()
        self.TT(sel, gt01, le, ALU.add)
        self.BAR()
        self.dbg("sel", sel, [128, 32])
        CS = self.SCR[:, 128:160]
        self._cumsum_inc(sel, CS)
        self.BAR()
        slotf = self.SCR[:, 160:192]
        self.TS(slotf, CS, -1.0, ALU.add)
        self.BAR()
        self.TT(slotf, slotf, sel, ALU.mult)
        ob = self.SCR[:, 192:224]
        self.TS(ob, sel, -1.0e9, ALU.mult, 1.0e9, ALU.add)
        self.BAR()
        self.TT(slotf, slotf, ob, ALU.add)
        self.BAR()
        slotu = self.IINT.bitcast(U32)[:, 0:32]
        self.COPY(slotu, slotf)
        self.BAR()
        for c in range(32):
            self.ISCATTER(self.s_idx.ap()[:, :], slotu[:, c:c + 1],
                          self.POSU[:, c:c + 1], KSEL - 1)
        self.BAR()
        self.dma(self.IDXU[:, 0:4],
                 self.s_idx.ap()[0:1, :].rearrange("one (k p) -> (one p) k", p=128), slow=True)
        self.BAR()
        self.COPY(self.PVAL[:, 0:4], self.IDXU[:, 0:4])
        self.BAR()
        self.dbg("posval", self.PVAL[:, 0:4], [128, 4])
        self.dbg("idxu", self.IDXU[:, 0:4], [128, 4], U32)
        minp = S[0:1, 10:11]
        self.COPY(minp, self.PVAL[0:1, 0:1])
        self.BAR()
        self.MM(self.PS_B[:, 256:257], self.ONESF[0:1, :], minp, True, True)
        self.BAR()
        minpb = S[:, 16:17]
        self.COPY(minpb, self.PS_B[:, 256:257])
        self.BAR()
        degrow = self.SCR[:, 0:HALF]
        self.TS(degrow, self.QP[:, :], minpb, ALU.is_ge)
        self.BAR()
        self.TT(self.COSFB[:, :], self.COSFB[:, :], degrow, ALU.mult)
        self.TT(self.SINFB[:, :], self.SINFB[:, :], degrow, ALU.mult)
        self.BAR()
        for kc in range(4):
            for tcg in range(4):
                kk = self.KEEP[:, (kc * 4 + tcg) * 512:(kc * 4 + tcg + 1) * 512]
                qslab = self.QP[:, tcg * 512:(tcg + 1) * 512]
                self.TS(self.SCR[:, 0:512], qslab, self.PVAL[:, kc:kc + 1], ALU.is_ge)
                self.TS(self.SCR[:, 512:1024], qslab, minpb, ALU.is_lt)
                self.BAR()
                self.TT(kk, self.SCR[:, 0:512], self.SCR[:, 512:1024], ALU.max)
                self.BAR()
        if getattr(self, "attn_stop", 99) <= 3:
            return
        self.P.scope = "a_gather"
        idl = self.PVAL[:, 4:8]
        self.TS(idl, self.PVAL[:, 0:4], self.METAB[:, 0:1], ALU.subtract)
        self.BAR()
        neg = self.PVAL[:, 8:12]
        self.TS(neg, idl, 0.0, ALU.is_lt)
        self.BAR()
        self.STT(idl, neg, 1.0e9, ALU.mult, idl, ALU.add)
        self.BAR()
        self.COPY(self.IDXU[:, 4:8], idl)
        self.BAR()
        XS = self.XT2
        for kc in range(4):
            self.MEMSET(XS[:, kc * D:(kc + 1) * D], 0.0)
        self.BAR()
        for kc in range(4):
            self.IGATHER(XS[:, kc * D:(kc + 1) * D], self.s_x1.ap()[:, :],
                         self.IDXU[:, 4 + kc:5 + kc], HALF - 1, tag="xsel")
        for kc in range(4):
            self.IGATHER(self.PVAL[:, 8 + kc:9 + kc], self.s_impfull.ap()[:, :],
                         self.IDXU[:, kc:kc + 1], L - 1, tag="gate")
        self.BAR()
        gcol = self.PVAL[:, 12:16]
        self.ACT(gcol, self.PVAL[:, 8:12], AF.Sigmoid)
        self.BAR()
        if getattr(self, "attn_stop", 99) <= 4:
            return
        self.P.scope = "a_agx"
        for kc in range(4):
            self.dma(self.cc_xs_in.ap()[kc * 128:(kc + 1) * 128, :],
                     XS[:, kc * D:(kc + 1) * D])
        self.BAR()
        self.CCALL("AllGather", ALU.bypass, self.cc_xs_in.ap()[:, :].opt(),
                   self.cc_xs_out.ap()[:, :].opt())
        pbase = self.PVAL[:, 8:9]
        self.COPY(pbase, self.POSALL[:, 0:1])
        self.BAR()
        for kc in range(4):
            self.TS(self.PVAL[:, 9:10], self.METAB[:, 1:2], 512.0, ALU.mult,
                    float(kc * 128), ALU.add)
            self.BAR()
            self.TT(self.PVAL[:, 9:10], self.PVAL[:, 9:10], pbase, ALU.add)
            self.BAR()
            self.COPY(self.IDXU[:, 8:9], self.PVAL[:, 9:10])
            self.BAR()
            self.IGATHER(self.WS[:, 0:D], self.cc_xs_out.ap()[:, :],
                         self.IDXU[:, 8:9], 8 * KSEL - 1, tag="merge")
            self.BAR()
            self.TT(XS[:, kc * D:(kc + 1) * D], XS[:, kc * D:(kc + 1) * D],
                    self.WS[:, 0:D], ALU.add)
            self.BAR()
        self.dbg("xsel", XS[:, 0:D], [128, D], BF16)

        if getattr(self, "attn_stop", 99) <= 5:
            return
        KTVB = self.POOLA.bitcast(BF16)
        KT0 = 1024
        VB0 = 3072
        EK0 = 5152
        RB0 = 7200
        for kc in range(4):
            ang = self.SCR[:, 0:64]
            self.TS(ang, self.INVB[:, :], self.PVAL[:, kc:kc + 1], ALU.mult)
            self.BAR()
            self._sincos(ang, 128, 64, self.SCR[:, 1024 + 64:1024 + 128],
                         self.SCR[:, 1024:1024 + 64], scr=self.SCR[:, 256:1024])
            self.BAR()
            self.TRANS(self.PS_TF[:, kc * 128:(kc + 1) * 128],
                       self.SCR[:, 1024:1024 + 128], self.IDENT)
            self.BAR()
        self.COPY(self.POOLA[:, 0:512], self.PS_TF[:, 0:512])
        self.BAR()
        C2S = self.XH[:, 0:512]
        S2S = self.XH[:, 512:1024]
        self.COPY(C2S[0:64, :], self.POOLA[0:64, 0:512])
        self.COPY(S2S[64:128, :], self.POOLA[64:128, 0:512])
        self.BAR()
        self.dma(C2S[64:128, :], C2S[0:64, :])
        self.dma(S2S[0:64, :], S2S[64:128, :])
        self.BAR()
        self.TS(S2S[0:64, :], S2S[0:64, :], -1.0, ALU.mult)
        self.BAR()
        self.P.scope = "a_kv"
        XST = self.XT2[:, 8192:16384]
        xst3 = XST.rearrange("p (dc n) -> p dc n", dc=16)
        for kc in range(4):
            for g in range(2):
                for j in range(8):
                    dc = g * 8 + j
                    self.TRANS(self.PS_T[:, j * 128:(j + 1) * 128],
                               XS[:, kc * D + dc * 128:kc * D + (dc + 1) * 128])
                self.BAR()
                self.COPY(xst3[:, g * 8:(g + 1) * 8, kc * 128:(kc + 1) * 128],
                          self.PS_T[:, 0:1024].rearrange("p (j n) -> p j n", j=8))
                self.BAR()
        ktok = {}

        def ldkv(m):
            half = m % 2
            ktok[m] = self.dma(self.WS[:, half * 2048:(half + 1) * 2048]
                               .rearrange("p (c n) -> p c n", c=16),
                               e["Wkv"].ap()[:, m * 128:(m + 1) * 128]
                               .rearrange("(c p) n -> p c n", p=128),
                               eng="gpsimd", nb=True)
        ldkv(0)
        self.BAR(fence4=ktok[0])
        for mq in range(2):
            for m4 in range(4):
                m = mq * 4 + m4
                if m + 1 < 8:
                    ldkv(m + 1)
                wsb = self.WS[:, (m % 2) * 2048:(m % 2 + 1) * 2048]
                for dc in range(DC):
                    self.MM(self.PS_A[:, m4 * 512:(m4 + 1) * 512],
                            wsb[:, dc * 128:(dc + 1) * 128], XST[:, dc * 512:(dc + 1) * 512],
                            dc == 0, dc == DC - 1)
                self.BAR(fence4=ktok.get(m + 1))
            dstkv = self.SCR[:, 0:D] if mq == 0 else self.WSF[:, 0:D]
            self.COPY(dstkv, self.PS_A[:, 0:2048])
            self.BAR()
        self.P.scope = "a_krope"
        for kvh in range(KVH):
            kk = self.SCR[:, kvh * 512:(kvh + 1) * 512]
            ksw = self.SMALL[:, 0:512]
            self.dma(ksw[0:64, :], kk[64:128, :])
            self.dma(ksw[64:128, :], kk[0:64, :])
            self.BAR()
            self.TT(ksw, ksw, S2S, ALU.mult)
            self.BAR()
            self.TT(kk, kk, C2S, ALU.mult)
            self.BAR()
            self.TT(KTVB[:, KT0 + kvh * 512:KT0 + (kvh + 1) * 512], kk, ksw, ALU.add)
            self.BAR()
        self.dbg("krope", KTVB[:, KT0:KT0 + 2048], [128, 2048], BF16)
        self.P.scope = "a_v"
        for kvh in range(KVH):
            for kc in range(4):
                self.TRANS(self.PS_TF[:, kc * 128:(kc + 1) * 128],
                           self.WSF[:, kvh * 512 + kc * 128:kvh * 512 + (kc + 1) * 128],
                           self.IDENT)
            self.BAR()
            for kc in range(4):
                vcol = VB0 + (kvh * 4 + kc) * 130
                self.TS(KTVB[:, vcol:vcol + 128], self.PS_TF[:, kc * 128:(kc + 1) * 128],
                        gcol[:, kc:kc + 1], ALU.mult)
                self.MEMSET(KTVB[:, vcol + 128:vcol + 129], 1.0)
            self.BAR()

        if getattr(self, "attn_stop", 99) <= 6:
            return
        self.P.scope = "a_q"
        QT = self.XT2
        WSFB = self.WSF.bitcast(BF16)

        qtok = {}

        def ldq(h):
            half = h % 2
            qtok[h] = self.dma(self.WS[:, half * 2048:(half + 1) * 2048]
                               .rearrange("p (c n) -> p c n", c=16),
                               e["Wq"].ap()[:, h * 128:(h + 1) * 128]
                               .rearrange("(c p) n -> p c n", p=128),
                               eng="gpsimd", nb=True)
        ldq(0)
        self.BAR(fence4=qtok[0])
        for h in range(H):
            if h + 1 < H:
                ldq(h + 1)
            wsb = self.WS[:, (h % 2) * 2048:(h % 2 + 1) * 2048]
            for tcg in range(4):
                ps = self.PS_A[:, tcg * 512:(tcg + 1) * 512]
                for dc in range(DC):
                    self.MM(ps, wsb[:, dc * 128:(dc + 1) * 128],
                            self.XT[:, dc * D + tcg * 512:dc * D + (tcg + 1) * 512],
                            dc == 0, dc == DC - 1)
            self.BAR()
            qz = self.XH[:, 0:D]
            self.COPY(qz, self.PS_A[:, 0:2048])
            self.BAR()
            qsw = WSFB[:, 0:D]
            self.dma(qsw[0:64, :], qz[64:128, :])
            self.dma(qsw[64:128, :], qz[0:64, :])
            self.BAR()
            t1 = WSFB[:, D:2 * D]
            self.TT(t1, qz, self.COSFB[:, :], ALU.mult)
            self.BAR()
            self.TT(qsw, qsw, self.SINFB[:, :], ALU.mult)
            self.BAR()
            self.TT(QT[:, h * D:(h + 1) * D], t1, qsw, ALU.add)
            self.BAR(fence4=qtok.get(h + 1))
        self.dbg("qrope", QT[:, 0:D], [128, D], BF16)

        if getattr(self, "attn_stop", 99) <= 7:
            return
        self.P.scope = "a_scores"
        AOT = self.XT
        NT = H * 4

        def op_s(t):
            h, tcg, kvh = t // 4, t % 4, t // 16
            for kc in range(4):
                self.MM(self.PS_A[:, kc * 512:(kc + 1) * 512],
                        KTVB[:, KT0 + kvh * 512 + kc * 128:KT0 + kvh * 512 + (kc + 1) * 128],
                        QT[:, h * D + tcg * 512:h * D + (tcg + 1) * 512],
                        True, True)

        def op_e(t):
            for kc in range(4):
                self.ACT(KTVB[:, EK0 + kc * 512:EK0 + (kc + 1) * 512],
                         self.PS_A[:, kc * 512:(kc + 1) * 512], AF.Exp,
                         scale=SCALE, bias=-SHIFT)

        def op_k(t):
            tcg = t % 4
            for kc in range(4):
                ekk = KTVB[:, EK0 + kc * 512:EK0 + (kc + 1) * 512]
                self.TT(ekk, ekk,
                        self.KEEP[:, (kc * 4 + tcg) * 512:(kc * 4 + tcg + 1) * 512],
                        ALU.mult)

        def op_p(t):
            kvh = t // 16
            for kc in range(4):
                vcol = VB0 + (kvh * 4 + kc) * 130
                self.MM(self.PS_B[:, 0:512], KTVB[:, vcol:vcol + 128],
                        KTVB[:, EK0 + kc * 512:EK0 + (kc + 1) * 512], kc == 0, kc == 3)
            for kc in range(4):
                vcol = VB0 + (kvh * 4 + kc) * 130
                self.MM(self.PS_B[0:1, 512:1024], KTVB[:, vcol + 128:vcol + 129],
                        KTVB[:, EK0 + kc * 512:EK0 + (kc + 1) * 512], kc == 0, kc == 3)

        def op_r(t):
            self.P.emit("vector", lambda E, _d=self.PS_B[0:1, 512:1024]:
                        E.reciprocal(out=self.SCR[0:1, 0:512], in_=_d))

        def op_b(t):
            self.MM(self.PS_TF[:, 0:512], self.ONESF[0:1, :],
                    self.SCR[0:1, 0:512], True, True)

        def op_v(t):
            self.ACT(self.SMALL[:, 0:512], self.PS_B[:, 0:512], AF.Identity)

        def op_a(t):
            h, tcg = t // 4, t % 4
            self.TT(AOT[:, h * D + tcg * 512:h * D + (tcg + 1) * 512],
                    self.SMALL[:, 0:512], self.PS_TF[:, 0:512], ALU.mult)

        for step in range(-2, NT + 1):
            if 0 <= step < NT:
                op_p(step)
            if 0 <= step - 1 < NT:
                op_a(step - 1)
            self.BAR()
            if 0 <= step + 1 < NT:
                op_e(step + 1)
            if 0 <= step < NT:
                op_r(step)
            self.BAR()
            if 0 <= step + 2 < NT:
                op_s(step + 2)
            if 0 <= step + 1 < NT:
                op_k(step + 1)
            if 0 <= step < NT:
                op_b(step)
                op_v(step)
            self.BAR()
        self.dbg("aot", AOT[:, 0:D], [128, D], BF16)

        if getattr(self, "attn_stop", 99) <= 8:
            return
        self.P.scope = "a_wo"
        WSFB2 = self.WSF.bitcast(BF16)
        wtokh = {}
        stokh = {}

        def ldwo(slab):
            buf = self.WS if slab % 2 == 0 else WSFB2
            wtokh[slab] = self.dma(
                buf[:, 0:4096].rearrange("p (c n) -> p c n", c=16),
                e["Wo"].ap()[:, slab * 256:(slab + 1) * 256]
                .rearrange("(c p) n -> p c n", p=128), eng="gpsimd", nb=True)
        ldwo(0)
        self.BAR(fence4=wtokh[0])
        for slab in range(8):
            buf = self.WS if slab % 2 == 0 else WSFB2
            for tc in range(TC):
                ps = self.PS_A[:, (tc % 8) * 256:(tc % 8 + 1) * 256]
                for dc in range(DC):
                    self.MM(ps, AOT[:, dc * D + tc * 128:dc * D + (tc + 1) * 128],
                            buf[:, dc * 256:(dc + 1) * 256], dc == 0, dc == DC - 1)
                if tc % 8 == 7:
                    sk = stokh.get((slab - (1 if tc == 7 else 0),
                                    15 if tc == 7 else 7))
                    self.BAR(fence3=sk if NBWO else None)
                    for t2 in range(tc - 7, tc + 1):
                        self.COPY(self.SCR[:, (t2 % 8) * 256:(t2 % 8 + 1) * 256],
                                  self.PS_A[:, (t2 % 8) * 256:(t2 % 8 + 1) * 256])
                    self.BAR()
                    for t2 in range(tc - 7, tc + 1):
                        stokh[(slab, tc)] = self.dma(
                            self.s_wo.ap()[t2 * 128:(t2 + 1) * 128,
                                           slab * 256:(slab + 1) * 256],
                            self.SCR[:, (t2 % 8) * 256:(t2 % 8 + 1) * 256], nb=NBWO)
                    if tc == 7 and slab + 1 < 8:
                        ldwo(slab + 1)
                    if tc == 15:
                        self.BAR(fence4=wtokh.get(slab + 1))
                    else:
                        self.BAR()
        self.P.scope = "a_out2"
        tco = [15] + list(range(15))

        xt2f = self.XT2.bitcast(F32)

        def o2bufs(i):
            p = i % 3
            if p == 0:
                return self.SCR[:, 0:D], self.WSF[:, 0:D]
            if p == 1:
                return self.POOLA[:, 0:D], self.POOLA[:, D:2 * D]
            return xt2f[:, 0:D], xt2f[:, D:2 * D]

        otok = {}
        wtok = {}

        def ldo2(i):
            tc = tco[i]
            b1, b2 = o2bufs(i)
            self.dma(b1, self.s_wo.ap()[tc * 128:(tc + 1) * 128, :], nb=NBO2)
            otok[i] = self.dma(b2, self.s_out1.ap()[tc * 128:(tc + 1) * 128, :],
                               nb=NBO2)
        def _f(v):
            return v if NBO2 else None
        self.BAR(fence3=self.P.ds3)
        ldo2(0)
        self.BAR(fence3=_f(otok[0]))
        for i, tc in enumerate(tco):
            b1, b2 = o2bufs(i)
            if i + 1 < TC:
                ldo2(i + 1)
            self.TT(b1, b1, b2, ALU.add)
            self.BAR()
            wtok[i] = self.dma(self.s_out1.ap()[tc * 128:(tc + 1) * 128, :], b1,
                               nb=NBO2)
            cands = [x for x in (otok.get(i + 1), wtok.get(i - 1)) if x is not None]
            self.BAR(fence3=_f(max(cands)) if cands else None)
            if tc == 0:
                self.dbg("out2t0", b1, [128, D])
            if tc == 15:
                self.dma(self.cc_sm_in.ap()[0:1, 0:D], self.SCR[127:128, 0:D])
                self.BAR()
                self.CCALL("AllGather", ALU.bypass, self.cc_sm_in.ap()[:, :].opt(),
                           self.cc_sm_out.ap()[:, :].opt())
                for bb in range(B):
                    self.dma(self.POOLB[bb:bb + 1, D:2 * D],
                             self.cc_sm_out.ap()[2 * bb + 1:2 * bb + 2, 0:D])
                self.BAR()
        self.BAR(fence3=self.P.ds3, fence4=self.P.ds4)

    def _cumsum_inc(self, src01, dst):
        self.MM(self.PS_B[:, 0:32], self.UT[:, :], src01, True, True)
        self.BAR()
        pre = self.SCR[:, 224:256]
        self.COPY(pre, self.PS_B[:, 0:32])
        self.BAR()
        self.dma(self.SCR[0:1, 320:352], pre[127:128, 0:32])
        self.BAR()
        self.TRANS(self.PS_TF[0:32, 32:33], self.SCR[0:1, 320:352], self.IDENT[0:1, 0:1])
        self.BAR()
        colsum = self.SCR[0:32, 256:257]
        self.COPY(colsum, self.PS_TF[0:32, 32:33])
        self.BAR()
        self.MM(self.PS_B[0:32, 33:34], self.SUT32[:, :], colsum, True, True)
        self.BAR()
        offs = self.SCR[0:32, 257:258]
        self.COPY(offs, self.PS_B[0:32, 33:34])
        self.BAR()
        self.TRANS(self.PS_TF[0:1, 34:66], offs, self.IDENT[0:32, 0:32])
        self.BAR()
        self.COPY(self.SCR[0:1, 258:290], self.PS_TF[0:1, 34:66])
        self.BAR()
        self.MM(self.PS_B[:, 64:96], self.ONESF[0:1, :], self.SCR[0:1, 258:290],
                True, True)
        self.BAR()
        offsb = self.SCR[:, 290:322]
        self.COPY(offsb, self.PS_B[:, 64:96])
        self.BAR()
        self.TT(dst, pre, offsb, ALU.add)
        self.BAR()


def _mk_cst():
    cst = np.zeros((128, 512), np.float32)
    cst[:, 0:128] = np.eye(128, dtype=np.float32)
    i = np.arange(128)
    cst[:, 128:256] = (i[:, None] <= i[None, :]).astype(np.float32)
    j = np.arange(32)
    cst[0:32, 320:352] = (j[:, None] < j[None, :]).astype(np.float32)
    return cst


_NC_CACHE = {}


def _get_nc(debug=(), phases=("ph0", "ffn1", "attn", "ffn3")):
    key = (tuple(sorted(debug)), tuple(sorted(phases)))
    if key not in _NC_CACHE:
        kb = KB(debug, phases)
        nc = kb.build()
        _NC_CACHE[key] = (nc, kb.dbg_names)
    return _NC_CACHE[key]


def make_in_maps(inputs):
    f32 = np.float32
    x = np.ascontiguousarray(inputs["L0_attn_x"], f32)
    vecs = np.zeros((16, D), f32)
    vecs[VB1, 0:512] = inputs["L0_attn_b1"]
    vecs[VGATEB] = inputs["L0_ffn_gate_b"]
    vecs[VAGGB] = inputs["L0_ffn_agg_b"]
    vecs[VBCB] = inputs["L0_ffn_bc_b"]
    vecs[VD1B, 0:DHS] = 0
    vecs[VD2B] = inputs["L0_ffn_d2_b"]
    vecs[VING] = inputs["L0_ffn_in_gamma"]
    vecs[VINB] = inputs["L0_ffn_in_beta"]
    vecs[VPNG] = inputs["L0_ffn_pn_gamma"]
    vecs[VPNB] = inputs["L0_ffn_pn_beta"]
    vecs[VOUTG] = inputs["L0_ffn_out_gamma"]
    vecs[VOUTB] = inputs["L0_ffn_out_beta"]
    cst = _mk_cst()
    inv = (1.0 / (f32(10000.0) ** (np.arange(0, 128, 2, dtype=f32) / f32(128)))).astype(f32)
    in_maps = []
    for c in range(8):
        b, hf = c // 2, c % 2
        goff = hf * HALF
        im = {}
        im["x_own"] = np.ascontiguousarray(x[b, goff:goff + HALF, :])
        im["x_last"] = np.ascontiguousarray(x[:, L - 1, :])
        im["Wq"] = inputs["L0_attn_Wq"]
        im["Wkv"] = inputs["L0_attn_Wkv"]
        im["Wo"] = inputs["L0_attn_Wo"]
        im["W1"] = inputs["L0_attn_W1"]
        im["gate_w"] = inputs["L0_ffn_gate_w"]
        im["d1s"] = np.ascontiguousarray(inputs["L0_ffn_d1_w"][:, c * DHS:(c + 1) * DHS])
        im["d2s"] = np.ascontiguousarray(inputs["L0_ffn_d2_w"][c * DHS:(c + 1) * DHS, :])
        im["aggs"] = np.ascontiguousarray(inputs["L0_ffn_agg_w"][:, c * 256:(c + 1) * 256])
        im["bcs"] = np.ascontiguousarray(inputs["L0_ffn_bc_w"][:, c * 256:(c + 1) * 256])
        v = vecs.copy()
        v[VD1B, 0:DHS] = inputs["L0_ffn_d1_b"][c * DHS:(c + 1) * DHS]
        im["vecs"] = v
        cstc = cst.copy()
        pos = (np.arange(4096) % 128 * 0 + np.arange(4096))
        posall = np.arange(L).reshape(32, 128).T.astype(f32)
        cstc[:, 256:288] = posall
        im["cst"] = cstc
        rows = np.zeros((1, 8192), f32)
        rows[0, 0:HALF] = np.arange(goff, goff + HALF, dtype=f32)
        rows[0, 2048:2048 + L] = np.arange(L, dtype=f32)
        rows[0, 6144:6208] = inv
        rows[0, 6208:6216] = [goff, c ^ 1, c, 0, b, float(inputs["L0_attn_b2"][0]),
                              2 * b, 2 * b + 1]
        rows[0, 6216:6216 + 512] = np.asarray(inputs["L0_attn_W2"], f32).ravel()
        im["rows"] = rows
        in_maps.append({k: np.ascontiguousarray(np.asarray(v2, f32)) for k, v2 in im.items()})
    return in_maps


def kernel(**inputs):
    nc, _ = _get_nc()
    in_maps = make_in_maps(inputs)
    res = run_bass_kernel_spmd(nc, in_maps, core_ids=list(range(8)))
    out = np.zeros((B, L, D), np.float32)
    for c in range(8):
        b, hf = c // 2, c % 2
        out[b, hf * HALF:(hf + 1) * HALF, :] = res.results[c]["out_own"]
    return out


# version 16 (final)
# speedup vs baseline: 1.5149x; 1.0075x over previous
import math
import contextlib
import numpy as np
import ml_dtypes
import concourse.bass as bass
import concourse.mybir as mybir
from concourse.bass import IndirectOffsetOnAxis
from concourse.bass_utils import run_bass_kernel_spmd

F32 = mybir.dt.float32
BF16 = mybir.dt.bfloat16
U32 = mybir.dt.uint32
I32 = mybir.dt.int32
AF = mybir.ActivationFunctionType
ALU = mybir.AluOpType
AX = mybir.AxisListType
nbf16 = ml_dtypes.bfloat16

B, L, D = 4, 4096, 2048
HALF, TC, DC = 2048, 16, 16
H, KVH, HD, KSEL = 16, 4, 128, 512
DH, DHS = 8192, 1024
SCALE = float(D // H) ** -0.5
SHIFT = 30.0
EPS5, EPS6 = 1e-5, 1e-6
NSEARCH = 38
ENGS = ("sync", "tensor", "vector", "scalar", "gpsimd")

VB1, VGATEB, VAGGB, VBCB, VD1B, VD2B = 0, 1, 2, 3, 4, 5
VING, VINB, VPNG, VPNB, VOUTG, VOUTB, VB2, VMETA = 6, 7, 8, 9, 10, 11, 12, 13


def _cw():
    tp = 2.0 * math.pi
    c1 = float(np.float32(6.28125))
    c2 = float(np.float32(round((tp - c1) * (1 << 21)) / (1 << 21)))
    c3 = float(np.float32(np.float64(tp) - c1 - c2))
    return c1, c2, c3


CW1, CW2, CW3 = _cw()
NBO2 = False
NBWO = False
STRICT = False


class Prog:
    def __init__(self):
        self.streams = {e: [] for e in ENGS}
        self.dirty = {e: False for e in ENGS}
        self.last_compute = {e: None for e in ENGS}
        self.inc_map = {}
        self.sb = 0
        self.ds = 0
        self.ds2 = 0
        self.ds3 = 0
        self.ds4 = 0
        self.f3 = 0
        self.f4 = 0
        self.cc = 0
        self.SB = self.DS = self.DS2 = self.DS3 = self.DS4 = self.CC = None

    def emit(self, eng, fn, compute=True):
        fn.scope = getattr(self, "scope", None)
        self.streams[eng].append(fn)
        self.dirty[eng] = True
        if compute:
            self.last_compute[eng] = len(self.streams[eng]) - 1

    def barrier(self, fence3=None, fence4=None):
        if STRICT:
            fence3, fence4 = self.ds3, self.ds4
        if fence3 is not None:
            self.f3 = max(self.f3, fence3)
        if fence4 is not None:
            self.f4 = max(self.f4, fence4)
        for e in ENGS:
            if self.dirty[e]:
                li = self.last_compute[e]
                if li is not None and (e, li) not in self.inc_map:
                    self.sb += 1
                    self.inc_map[(e, li)] = 1
                self.dirty[e] = False
                self.last_compute[e] = None
        sbt, dst, dst2 = self.sb, self.ds, self.ds2
        f3t, f4t = self.f3, self.f4
        for e in ENGS:
            def w(E):
                raise AssertionError("wait marker must not be called")
            w.wait_info = (sbt, dst, dst2, f3t, f4t)
            w.scope = getattr(self, "scope", None)
            self.streams[e].append(w)

    def replay(self, block, nc=None):
        s = self.streams
        inc_map = self.inc_map
        sems = [self.SB, self.DS, self.DS2, self.DS3, self.DS4]

        def run(E, name):
            stream = s[name]
            n_st = len(stream)
            last = [0, 0, 0, 0, 0]
            cur_scope = [None]
            ctx = [None]

            def set_scope(sc):
                if sc == cur_scope[0] or nc is None:
                    return
                if ctx[0] is not None:
                    ctx[0].__exit__(None, None, None)
                    ctx[0] = None
                if sc is not None:
                    ctx[0] = nc.named_scope(sc)
                    ctx[0].__enter__()
                cur_scope[0] = sc

            for i, f in enumerate(stream):
                wi = getattr(f, "wait_info", None)
                if wi is not None:
                    if i + 1 < n_st and getattr(stream[i + 1], "wait_info", None):
                        continue
                    set_scope(getattr(f, "scope", None))
                    for k, v in enumerate(wi):
                        if v > last[k]:
                            E.wait_ge(sems[k], v)
                            last[k] = v
                    continue
                set_scope(getattr(f, "scope", None))
                inst = f(E)
                n = inc_map.get((name, i))
                if n:
                    inst.then_inc(self.SB, n)
            set_scope(None)

        @block.sync
        def _(E):
            run(E, "sync")

        @block.tensor
        def _(E):
            run(E, "tensor")

        @block.vector
        def _(E):
            run(E, "vector")

        @block.scalar
        def _(E):
            run(E, "scalar")

        @block.gpsimd
        def _(E):
            run(E, "gpsimd")


class KB:
    def __init__(self, debug=(), phases=("ph0", "ffn1", "attn", "ffn3")):
        self.phases = set(phases)
        self.debug = set(debug)
        self.dbg_names = []
        self.nc = bass.Bass("TRN2", target_bir_lowering=False)
        self.P = Prog()
        self.stack = contextlib.ExitStack()
        self._io()

    def _io(self):
        nc = self.nc
        self.ein = {}
        for name, shape in [
            ("x_own", [HALF, D]), ("x_last", [B, D]),
            ("Wq", [D, D]), ("Wkv", [D, 1024]), ("Wo", [D, D]),
            ("W1", [D, 512]), ("gate_w", [D, D]),
            ("d1s", [D, DHS]), ("d2s", [DHS, D]),
            ("aggs", [D, 256]), ("bcs", [D, 256]),
            ("vecs", [16, D]), ("cst", [128, 512]), ("rows", [1, 8192]),
        ]:
            self.ein[name] = nc.dram_tensor(name, shape, F32, kind="ExternalInput")
        self.out_own = nc.dram_tensor("out_own", [HALF, D], F32, kind="ExternalOutput")
        d = nc.dram_tensor
        self.s_xpre = d("s_xpre", [HALF, D], BF16)
        self.s_x1 = d("s_x1", [HALF, D], BF16)
        self.s_out1 = d("s_out1", [HALF, D], F32)
        self.s_ropetab = d("s_ropetab", [L, 128], F32)
        self.s_idx = d("s_idx", [1, KSEL], U32)
        self.s_impfull = d("s_impfull", [L, 1], F32)
        self.s_wo = d("s_wo", [HALF, D], F32)
        self.cc_th_in = d("cc_th_in", [B, D], F32)
        self.cc_th_out = d("cc_th_out", [8 * B, D], F32)
        self.cc_thr_out = d("cc_thr_out", [B, D], F32)
        self.cc_sm_in = d("cc_sm_in", [1, 2048], F32)
        self.cc_sm_out = d("cc_sm_out", [8, 2048], F32)
        self.cc_ag_in = d("cc_ag_in", [1, 1024], F32)
        self.cc_ag_out = d("cc_ag_out", [8, 1024], F32)
        self.cc_xs_in = d("cc_xs_in", [KSEL, D], BF16)
        self.cc_xs_out = d("cc_xs_out", [8 * KSEL, D], BF16)

    def sbuf(self, name, shape, dt):
        return self.stack.enter_context(self.nc.sbuf_tensor(name, list(shape), dt))

    def psumt(self, name, shape):
        return self.stack.enter_context(self.nc.psum_tensor(name, list(shape), F32))

    def dma(self, out, in_, eng="sync", slow=False, nb=False):
        P = self.P
        if eng == "gpsimd":
            if nb:
                DS = P.DS4
                P.ds4 += 16
                ret = P.ds4
            else:
                DS = P.DS2
                P.ds2 += 16
                ret = P.ds2
        else:
            if nb:
                DS = P.DS3
                P.ds3 += 16
                ret = P.ds3
            else:
                DS = P.DS
                P.ds += 16
                ret = P.ds
        if slow:
            P.emit(eng, lambda E: E.dma_start(
                out=out, in_=in_, allow_slow_non_contiguous=True).then_inc(DS, 16),
                compute=False)
        else:
            P.emit(eng, lambda E: E.dma_start(out=out, in_=in_).then_inc(DS, 16),
                   compute=False)
        if eng == "gpsimd":
            self._sw_tick(nb)
        return ret

    def TT(self, out, a, b, op, eng="vector"):
        self.P.emit(eng, lambda E: E.tensor_tensor(out=out, in0=a, in1=b, op=op))

    def TS(self, out, a, s1, op0, s2=None, op1=None, eng="vector"):
        if op1 is None:
            self.P.emit(eng, lambda E: E.tensor_scalar(
                out=out, in0=a, scalar1=s1, scalar2=None, op0=op0))
        else:
            self.P.emit(eng, lambda E: E.tensor_scalar(
                out=out, in0=a, scalar1=s1, scalar2=s2, op0=op0, op1=op1))

    def STT(self, out, a, s, op0, b, op1):
        self.P.emit("vector", lambda E: E.scalar_tensor_tensor(
            out=out, in0=a, scalar=s, in1=b, op0=op0, op1=op1))

    def ACT(self, out, in_, func, bias=None, scale=None):
        kw = dict(out=out, in_=in_, func=func)
        if bias is not None:
            kw["bias"] = bias
        if scale is not None:
            kw["scale"] = scale
        self.P.emit("scalar", lambda E: E.activation(**kw))

    def MM(self, out, lhsT, rhs, start, stop):
        self.P.emit("tensor", lambda E: E.matmul(out, lhsT, rhs, start=start, stop=stop))

    def TRANS(self, out, in_, ident=None):
        idn = ident if ident is not None else self.IDENTB
        idn = idn[:, :] if hasattr(idn, "ap") and not isinstance(idn, bass.AP) else idn
        self.P.emit("tensor", lambda E, _i=idn: E.transpose(out, in_, _i))

    def COPY(self, out, in_, eng="vector"):
        self.P.emit(eng, lambda E: E.tensor_copy(out=out, in_=in_))

    def RED(self, out, in_, op, eng="vector"):
        self.P.emit(eng, lambda E: E.tensor_reduce(out=out, in_=in_, axis=AX.X, op=op))

    def MEMSET(self, ap, v, eng="vector"):
        self.P.emit(eng, lambda E: E.memset(ap, v))

    def CCALL(self, kind, op, ins, outs):
        P = self.P
        P.cc += 1
        t = P.cc
        CC = P.CC
        P.emit("gpsimd", lambda E: E.collective_compute(
            kind, op, replica_groups=[list(range(8))],
            ins=[ins], outs=[outs]).then_inc(CC, 1), compute=False)
        P.emit("gpsimd", lambda E, _t=t: E.wait_ge(CC, _t), compute=False)
        P.emit("gpsimd", lambda E: E.engine_nop(), compute=True)
        self.BAR()

    def _sw_tick(self, nb=False):
        self._swn = getattr(self, "_swn", 0) + 1
        if self._swn % 2 == 0:
            P = self.P
            if nb:
                tgt = max(P.ds4 - 32, 0)
                if tgt:
                    P.emit("gpsimd", lambda E, _t=tgt: E.wait_ge(P.DS4, _t),
                           compute=False)
            else:
                tgt = P.ds2
                P.emit("gpsimd", lambda E, _t=tgt: E.wait_ge(P.DS2, _t), compute=False)

    def IGATHER(self, out, src, idxcol, bounds, tag=""):
        P = self.P
        DS = P.DS2
        P.ds2 += 16
        def f(E, _o=out, _s=src, _i=idxcol, _b=bounds, _t=tag):
            try:
                return E.indirect_dma_start(
                    out=_o, out_offset=None, in_=_s,
                    in_offset=IndirectOffsetOnAxis(ap=_i, axis=0),
                    bounds_check=_b, oob_is_err=False).then_inc(DS, 16)
            except Exception as ex:
                raise RuntimeError(f"IGATHER[{_t}] failed: {ex}") from ex
        P.emit("gpsimd", f, compute=False)
        self._sw_tick()

    def ISCATTER(self, out, slotcol, vals, bounds):
        P = self.P
        DS = P.DS2
        P.ds2 += 16
        P.emit("gpsimd", lambda E: E.indirect_dma_start(
            out=out, out_offset=IndirectOffsetOnAxis(ap=slotcol, axis=1),
            in_=vals, in_offset=None,
            bounds_check=bounds, oob_is_err=False).then_inc(DS, 16), compute=False)
        self._sw_tick()

    def BAR(self, fence3=None, fence4=None):
        self.P.barrier(fence3=fence3, fence4=fence4)

    def dbg(self, name, ap, shape, dt=F32):
        if name not in self.debug:
            return
        t = self.nc.dram_tensor("dbg_" + name, list(shape), dt, kind="ExternalOutput")
        self.dbg_names.append(name)
        self.dma(t.ap()[:, :], ap)
        self.BAR()

    def build(self):
        with self.stack:
            self.SB = self.stack.enter_context(self.nc.semaphore("SBAR"))
            self.DSm = self.stack.enter_context(self.nc.semaphore("DSEM"))
            self.DS2m = self.stack.enter_context(self.nc.semaphore("DSWSEM"))
            self.DS3m = self.stack.enter_context(self.nc.semaphore("DNB"))
            self.DS4m = self.stack.enter_context(self.nc.semaphore("DNBSW"))
            self.CCm = self.stack.enter_context(self.nc.semaphore("CCSEM"))
            self.P.SB, self.P.DS, self.P.CC = self.SB, self.DSm, self.CCm
            self.P.DS2 = self.DS2m
            self.P.DS3 = self.DS3m
            self.P.DS4 = self.DS4m
            self._alloc()
            for i, v in enumerate([EPS5, EPS6, SCALE, -SHIFT]):
                self.nc.const_aps.register(F32, float(v), self.CONSTC[:, i:i + 1])
            self.P.scope = "ph0"
            self._ph0()
            if "ffn1" in self.phases:
                self.P.scope = "ffn1"
                self._ffn(call=1)
            if "attn" in self.phases:
                self._attn()
            if "ffn3" in self.phases:
                self.P.scope = "ffn3"
                self._ffn(call=3)
            else:
                self.MEMSET(self.SCR[:, 0:D], 0.0)
                self.BAR()
                for tc in range(TC):
                    self.dma(self.out_own.ap()[tc * 128:(tc + 1) * 128, :], self.SCR[:, 0:D])
                self.BAR()
            self.P.barrier(fence3=self.P.ds3, fence4=self.P.ds4)
            with self.nc.Block() as block:
                self.P.replay(block, self.nc)
        return self.nc

    def _alloc(self):
        sb = self.sbuf
        self.IDENT = sb("IDENT", [128, 128], F32)
        self.IDENTB = sb("IDENTB", [128, 128], BF16)
        self.UT = sb("UT", [128, 128], F32)
        self.SUT32 = sb("SUT32", [32, 32], F32)
        self.POSALL = sb("POSALL", [128, 32], F32)
        self.POSU = sb("POSU", [128, 32], U32)
        self.COSFB = sb("COSFB", [128, HALF], BF16)
        self.SINFB = sb("SINFB", [128, HALF], BF16)
        self.METAB = sb("METAB", [128, 8], F32)
        self.STAT = sb("STAT", [128, 64], F32)
        self.SMALL = sb("SMALL", [128, 512], F32)
        self.SMALLB = sb("SMALLB", [128, 512], BF16)
        self.IMPS = sb("IMPS", [128, 40], F32)
        self.IDXU = sb("IDXU", [128, 16], U32)
        self.PVAL = sb("PVAL", [128, 16], F32)
        self.CONSTC = sb("CONSTC", [128, 8], F32)
        self.ONESB = sb("ONESB", [1, 128], BF16)
        self.ONESF = sb("ONESF", [1, 128], F32)
        self.HALFF = sb("HALFF", [1, 128], F32)
        self.XH = sb("XH", [128, D], BF16)
        self.INVB = sb("INVB", [128, 64], F32)
        self.XT = sb("XT", [128, TC * D], BF16)
        self.XT2 = sb("XT2", [128, TC * D], BF16)
        self.SCR = sb("SCR", [128, D], F32)
        self.WS = sb("WS", [128, 2 * D], BF16)
        self.WSF = sb("WSF", [128, D], F32)
        self.POOLA = sb("POOLA", [128, 2 * D], F32)
        self.POOLB = sb("POOLB", [128, 2 * D], F32)
        self.HB = self.POOLA.bitcast(BF16)
        self.QP = self.POOLA[:, 0:2048]
        self.KEEP = self.POOLB.bitcast(BF16)
        self.GAM = self.POOLA.bitcast(BF16)
        self.IINT = self.POOLB.bitcast(I32)
        self.PS_A = self.psumt("PS_A", [128, 2048])
        self.PS_B = self.psumt("PS_B", [128, 1024])
        self.PS_T = self.psumt("PS_T", [128, 1024], BF16) if False else None
        self.PS_T = self.stack.enter_context(self.nc.psum_tensor("PS_T", [128, 1024], BF16))
        self.PS_TF = self.psumt("PS_TF", [128, 512])

    def _ph0(self):
        e = self.ein
        stop = getattr(self, "ph0_stop", 99)
        for i, v in enumerate([EPS5, EPS6, SCALE, -SHIFT]):
            self.MEMSET(self.CONSTC[:, i:i + 1], float(v))
        self.dma(self.IDENT[:, :], e["cst"].ap()[:, 0:128])
        self.dma(self.UT[:, :], e["cst"].ap()[:, 128:256])
        self.dma(self.POSALL[:, :], e["cst"].ap()[:, 256:288])
        self.dma(self.SUT32[:, :], e["cst"].ap()[0:32, 320:352])
        self.BAR()
        self.COPY(self.IDENTB[:, :], self.IDENT[:, :])
        self.COPY(self.POSU[:, :], self.POSALL[:, :])
        self.MEMSET(self.ONESF[0:1, :], 1.0)
        self.MEMSET(self.HALFF[0:1, :], 0.5)
        self.MEMSET(self.ONESB[0:1, :], 1.0)
        self.BAR()
        self.dma(self.SCR[0:1, 0:8], e["rows"].ap()[0:1, 6208:6216])
        self.BAR()
        self.MM(self.PS_B[:, 0:8], self.ONESF[0:1, :], self.SCR[0:1, 0:8], True, True)
        self.BAR()
        self.COPY(self.METAB[:, :], self.PS_B[:, 0:8])
        if stop <= 1:
            return
        self.dma(self.SMALL[0:1, 448:512], e["rows"].ap()[0:1, 6144:6208])
        self.dma(self.SCR[0:1, 0:D], e["rows"].ap()[0:1, 0:D])
        self.BAR()
        inv = self.SMALL[0:1, 448:512]
        if stop <= 2:
            return
        for s in range(getattr(self, "fm_iters", 4)):
            self.MM(self.PS_B[0:64, 0:512], inv, self.SCR[0:1, s * 512:(s + 1) * 512],
                    True, True)
            self.BAR()
            self._sincos(self.PS_B[0:64, 0:512], 64, 512,
                         self.SINFB[64:128, s * 512:(s + 1) * 512],
                         self.COSFB[0:64, s * 512:(s + 1) * 512])
            self.BAR()
        if stop <= 3:
            return
        if getattr(self, "fm_dup", True) is False:
            return
        self.dma(self.COSFB[64:128, :], self.COSFB[0:64, :])
        self.dma(self.SINFB[0:64, :], self.SINFB[64:128, :])
        self.BAR()
        self.TS(self.SINFB[0:64, :], self.SINFB[0:64, :], -1.0, ALU.mult)
        self.BAR()
        if getattr(self, "fm_invb", True):
            self.MM(self.PS_B[:, 0:64], self.ONESF[0:1, :], inv, True, True)
            self.BAR()
            self.COPY(self.INVB[:, :], self.PS_B[:, 0:64])
            self.BAR()

    def _sincos(self, ang_psum, prows, n, out_sin, out_cos, scr=None):
        stop = getattr(self, "sc_stop", 99)
        scr = scr if scr is not None else self.POOLB
        a = scr[0:prows, 0:n]
        k = scr[0:prows, n:2 * n]
        ii = scr.bitcast(I32)[0:prows, 2 * n:3 * n]
        self.COPY(a, ang_psum)
        self.BAR()
        if stop <= 1:
            self.COPY(out_sin, a)
            self.COPY(out_cos, a)
            self.BAR()
            return
        self.TS(k, a, 1.0 / (2 * math.pi), ALU.mult)
        if stop <= 2:
            self.COPY(out_sin, k)
            self.COPY(out_cos, k)
            self.BAR()
            return
        self.COPY(ii, k)
        self.COPY(k, ii)
        if stop <= 3:
            self.COPY(out_sin, k)
            self.COPY(out_cos, k)
            self.BAR()
            return
        self.STT(a, k, -CW1, ALU.mult, a, ALU.add)
        self.STT(a, k, -CW2, ALU.mult, a, ALU.add)
        self.STT(a, k, -CW3, ALU.mult, a, ALU.add)
        self.TS(k, a, math.pi, ALU.is_ge)
        self.STT(k, k, -2 * math.pi, ALU.mult, a, ALU.add)
        self.BAR()
        if stop <= 4:
            self.COPY(out_sin, k)
            self.COPY(out_cos, k)
            self.BAR()
            return
        self.ACT(out_sin, k, AF.Sin)
        self.BAR()
        self.TS(a, a, math.pi / 2, ALU.add)
        self.TS(k, a, math.pi, ALU.is_ge)
        self.STT(k, k, -2 * math.pi, ALU.mult, a, ALU.add)
        self.BAR()
        self.ACT(out_cos, k, AF.Sin)
        self.BAR()

    def _ln_stats(self, src, prows, eps, r_col, nmr_col, base=0):
        S = self.STAT
        mean = S[0:prows, base + 24:base + 25]
        ex2 = S[0:prows, base + 25:base + 26]
        var = S[0:prows, base + 26:base + 27]
        mm2 = S[0:prows, base + 36:base + 37]
        sq = S[0:prows, base + 37:base + 38]
        self.RED(mean, src, ALU.add)
        self.ACT(self.PS_A[0:prows, 0:2048], src, AF.Square)
        self.BAR()
        self.TS(mean, mean, 1.0 / 2048.0, ALU.mult)
        self.RED(ex2, self.PS_A[0:prows, 0:2048], ALU.add)
        self.BAR()
        self.TS(ex2, ex2, 1.0 / 2048.0, ALU.mult)
        self.TT(mm2, mean, mean, ALU.mult)
        self.BAR()
        self.TT(var, ex2, mm2, ALU.subtract)
        self.BAR()
        self.ACT(sq, var, AF.Sqrt, bias=float(eps))
        self.BAR()
        self.P.emit("vector", lambda E, _r=r_col, _s=sq: E.reciprocal(out=_r, in_=_s))
        self.BAR()
        self.ACT(nmr_col, mean, AF.Identity, scale=r_col)
        self.BAR()
        self.TS(nmr_col, nmr_col, -1.0, ALU.mult)
        self.BAR()

    def _ffn(self, call):
        e = self.ein
        src_rows = e["x_own"].ap() if call == 1 else self.s_out1.ap()
        XPT = self.XT2 if call == 3 else self.XT
        G = self.XT if call == 3 else self.XT2
        r_c = self.STAT[:, 40:41]
        nm_c = self.STAT[:, 41:42]

        for j, vec in ((0, VOUTG), (1, VOUTB)):
            self.dma(self.SCR[0:1, 0:D], e["vecs"].ap()[vec:vec + 1, :])
            self.BAR()
            for s in range(4):
                self.MM(self.PS_A[:, s * 512:(s + 1) * 512],
                        self.HALFF[0:1, :],
                        self.SCR[0:1, s * 512:(s + 1) * 512], True, True)
            self.BAR()
            self.COPY(self.GAM[:, j * D:(j + 1) * D], self.PS_A[:, 0:D])
            self.BAR()

        self.P.scope = f"f{call}A"
        for tc in range(TC):
            xt = self.SCR[:, 0:D]
            self.dma(xt, src_rows[tc * 128:(tc + 1) * 128, :])
            self.BAR()
            self._ln_stats(xt, 128, EPS5, r_c, nm_c)
            self.BAR()
            xpb = self.XH[:, 0:D]
            self.ACT(xpb, xt, AF.Identity, bias=nm_c, scale=r_c)
            self.BAR()
            self.dma(self.s_xpre.ap()[tc * 128:(tc + 1) * 128, :], xpb)
            xpt3 = XPT[:, 0:TC * D].rearrange("p (dc n) -> p dc n", dc=16)
            for g in range(2):
                for j in range(8):
                    dc = g * 8 + j
                    self.TRANS(self.PS_T[:, j * 128:(j + 1) * 128],
                               xpb[:, dc * 128:(dc + 1) * 128])
                self.BAR()
                self.COPY(xpt3[:, g * 8:(g + 1) * 8, tc * 128:(tc + 1) * 128],
                          self.PS_T[:, 0:1024].rearrange("p (j n) -> p j n", j=8))
                self.BAR()

        if getattr(self, "ffn_stop", 99) <= 1:
            return
        self.dbg(f"xpt{call}", XPT[:, 0:TC * D], [128, TC * D], BF16)
        self.P.scope = f"f{call}th"
        if call == 1:
            self.dma(self.POOLB[0:B, D:2 * D], e["x_last"].ap()[:, :])
            self.BAR()
        yl = self.POOLB[0:B, D:2 * D]
        yln = self.POOLB[0:B, 0:D]
        rl = self.STAT[0:B, 44:45]
        nl = self.STAT[0:B, 45:46]
        self._ln_stats(yl, B, EPS5, rl, nl, base=0)
        self.BAR()
        self.TS(yln, yl, rl, ALU.mult, nl, ALU.add)
        self.BAR()
        self._thought(G)
        self.dbg(f"bcb{call}", self.GAM[:, 2 * D:3 * D], [128, D], BF16)

        if getattr(self, "ffn_stop", 99) <= 2:
            return
        self.P.scope = f"f{call}gate"
        WSFB = self.WSF.bitcast(BF16)

        gtok = {}

        def ldgate(slab):
            buf = self.WS if slab % 2 == 0 else WSFB
            self.dma(buf[:, 0:4096].rearrange("p (c n) -> p c n", c=16),
                     e["gate_w"].ap()[:, slab * 256:(slab + 1) * 256]
                     .rearrange("(c p) n -> p c n", p=128), eng="gpsimd", nb=True)
            gtok[slab] = self.dma(
                self.SMALLB[0:1, (slab % 2) * 256:(slab % 2) * 256 + 256],
                e["vecs"].ap()[VGATEB:VGATEB + 1, slab * 256:(slab + 1) * 256],
                eng="gpsimd", nb=True)
        ldgate(0)
        self.BAR(fence4=gtok[0])
        for slab in range(8):
            buf = self.WS if slab % 2 == 0 else WSFB
            bcol = (slab % 2) * 256
            for tc in range(TC):
                ps = self.PS_A[:, (tc % 8) * 256:(tc % 8 + 1) * 256]
                for dc in range(DC):
                    self.MM(ps, XPT[:, dc * D + tc * 128:dc * D + (tc + 1) * 128],
                            buf[:, dc * 256:(dc + 1) * 256], dc == 0, False)
                self.MM(ps, self.ONESB[0:1, :], self.SMALLB[0:1, bcol:bcol + 256],
                        False, True)
                if tc % 8 == 7:
                    self.BAR()
                    for t2 in range(tc - 7, tc + 1):
                        self.ACT(G[:, t2 * D + slab * 256:t2 * D + (slab + 1) * 256],
                                 self.PS_A[:, (t2 % 8) * 256:(t2 % 8 + 1) * 256],
                                 AF.Sigmoid)
                    if tc == 15:
                        self.BAR(fence4=gtok.get(slab + 1))
                    else:
                        self.BAR()
                        if slab + 1 < 8:
                            ldgate(slab + 1)
        self.dbg(f"g{call}", G[:, 0:D], [128, D], BF16)

        if getattr(self, "ffn_stop", 99) <= 3:
            return
        self.P.scope = f"f{call}C"
        for tc in range(TC):
            xpb = self.XH[:, 0:D]
            self.dma(xpb, self.s_xpre.ap()[tc * 128:(tc + 1) * 128, :])
            xt = self.SCR[:, 0:D]
            self.dma(xt, src_rows[tc * 128:(tc + 1) * 128, :])
            self.BAR()
            u = self.WS[:, 0:D]
            self.TT(u, G[:, tc * D:(tc + 1) * D], self.GAM[:, 2 * D:3 * D], ALU.mult)
            self.BAR()
            self.TT(u, u, xpb, ALU.add)
            self.BAR()
            self._ln_stats(u, 128, EPS6, r_c, nm_c)
            self.BAR()
            xh = self.WS[:, D:2 * D]
            self.ACT(xh, u, AF.Identity, bias=nm_c, scale=r_c)
            self.BAR()
            t1 = self.XH[:, 0:D]
            self.TT(t1, xh, self.GAM[:, 0:D], ALU.mult)
            self.BAR()
            f2 = self.WS[:, 0:D]
            self.TT(f2, t1, self.GAM[:, D:2 * D], ALU.add)
            self.BAR()
            out = self.WSF[:, 0:D]
            self.TT(out, f2, xt, ALU.add)
            self.BAR()
            if call == 1:
                self.dma(self.s_out1.ap()[tc * 128:(tc + 1) * 128, :], out)
                self._ln_stats(out, 128, EPS5, r_c, nm_c)
                self.BAR()
                x1b = self.XH[:, 0:D]
                self.ACT(x1b, out, AF.Identity, bias=nm_c, scale=r_c)
                self.BAR()
                self.dma(self.s_x1.ap()[tc * 128:(tc + 1) * 128, :], x1b)
                x1t3 = self.XT[:, 0:TC * D].rearrange("p (dc n) -> p dc n", dc=16)
                for g in range(2):
                    for j in range(8):
                        dc = g * 8 + j
                        self.TRANS(self.PS_T[:, j * 128:(j + 1) * 128],
                                   x1b[:, dc * 128:(dc + 1) * 128])
                    self.BAR()
                    self.COPY(x1t3[:, g * 8:(g + 1) * 8, tc * 128:(tc + 1) * 128],
                              self.PS_T[:, 0:1024].rearrange("p (j n) -> p j n", j=8))
                    self.BAR()
            else:
                self.dma(self.out_own.ap()[tc * 128:(tc + 1) * 128, :], out)
        self.BAR()

    def _thought(self, G):
        e = self.ein
        yn = self.POOLB[0:B, 0:D]
        th = self.POOLB[0:B, D:2 * D]
        tn = self.POOLB[0:B, 0:D]
        vrow = self.WSF[0:B, 0:D]
        rl = self.STAT[0:B, 44:45]
        nl = self.STAT[0:B, 45:46]

        for dc in range(DC):
            self.dma(G[:, dc * 1024:(dc + 1) * 1024],
                     e["d1s"].ap()[dc * 128:(dc + 1) * 128, :], eng="gpsimd", nb=True)
        for kc in range(8):
            self.d12tok = self.dma(
                G[:, 16384 + kc * 2048:16384 + (kc + 1) * 2048],
                e["d2s"].ap()[kc * 128:(kc + 1) * 128, :], eng="gpsimd", nb=True)

        def vload(vec, n=D):
            for p in range(B):
                self.dma(vrow[p:p + 1, 0:n], e["vecs"].ap()[vec:vec + 1, 0:n])

        def ln_gb(src, dst, gvec, bvec):
            self._ln_stats(src, B, EPS6, rl, nl, base=0)
            self.BAR()
            self.TS(dst, src, rl, ALU.mult, nl, ALU.add)
            self.BAR()
            vload(gvec)
            self.BAR()
            self.TT(dst, dst, vrow, ALU.mult)
            self.BAR()
            vload(bvec)
            self.BAR()
            self.TT(dst, dst, vrow, ALU.add)
            self.BAR()

        def trows(src, width):
            nt = width // 128
            for j in range(nt):
                self.TRANS(self.PS_TF[:, j * 4:(j + 1) * 4],
                           src[:, j * 128:(j + 1) * 128], self.IDENT[0:B, 0:B])
            self.BAR()
            self.COPY(self.SMALLB[:, 0:nt * 4], self.PS_TF[:, 0:nt * 4])
            self.BAR()

        ln_gb(self.POOLB[0:B, 0:D], yn, VING, VINB)


        trows(yn, D)
        for g4 in range(4):
            self.dma(self.WS[:, g4 * 1024:g4 * 1024 + 1024]
                     .rearrange("p (c n) -> p c n", c=4),
                     e["aggs"].ap()[g4 * 512:(g4 + 1) * 512, :]
                     .rearrange("(c p) n -> p c n", p=128), eng="gpsimd")
        self.BAR()
        for dc in range(DC):
            self.MM(self.PS_A[0:B, 0:256], self.SMALLB[:, dc * 4:(dc + 1) * 4],
                    self.WS[:, dc * 256:(dc + 1) * 256], dc == 0, dc == DC - 1)
        self.BAR()
        self.COPY(self.SCR[0:B, 0:256], self.PS_A[0:B, 0:256])
        self.BAR()
        self.dma(self.cc_ag_in.ap()[0:1, :], self.SCR[0:B, 0:256])
        self.BAR()
        self.CCALL("AllGather", ALU.bypass, self.cc_ag_in.ap()[:, :].opt(),
                   self.cc_ag_out.ap()[:, :].opt())
        for r in range(8):
            self.dma(th[:, r * 256:(r + 1) * 256], self.cc_ag_out.ap()[r:r + 1, :])
        self.BAR()
        vload(VAGGB)
        self.BAR()
        self.TT(th, th, vrow, ALU.add)
        self.BAR()
        self.dbg("thinit", th, [B, D])

        self.BAR(fence4=self.d12tok)
        for step in range(4):
            ln_gb(th, tn, VPNG, VPNB)
            trows(tn, D)
            for dc in range(DC):
                for hh in range(2):
                    self.MM(self.PS_A[0:B, hh * 512:(hh + 1) * 512],
                            self.SMALLB[:, dc * 4:(dc + 1) * 4],
                            G[:, dc * 1024 + hh * 512:dc * 1024 + (hh + 1) * 512],
                            dc == 0, dc == DC - 1)
            self.BAR()
            hsl = self.POOLB[0:B, 0:DHS]
            vload(VD1B, DHS)
            self.BAR()
            self.TT(hsl, self.PS_A[0:B, 0:DHS], vrow[:, 0:DHS], ALU.add)
            self.BAR()
            self.ACT(hsl, hsl, AF.Gelu)
            self.BAR()
            trows(hsl, DHS)
            for kc in range(8):
                for qq in range(4):
                    self.MM(self.PS_A[0:B, qq * 512:(qq + 1) * 512],
                            self.SMALLB[:, kc * 4:(kc + 1) * 4],
                            G[:, 16384 + kc * 2048 + qq * 512:
                               16384 + kc * 2048 + (qq + 1) * 512],
                            kc == 0, kc == 7)
            self.BAR()
            self.COPY(self.SCR[0:B, 0:D], self.PS_A[0:B, 0:D])
            self.BAR()
            self.dma(self.cc_th_in.ap()[:, :], self.SCR[0:B, 0:D])
            self.BAR()
            self.CCALL("AllReduce", ALU.add, self.cc_th_in.ap()[:, :].opt(),
                       self.cc_thr_out.ap()[:, :].opt())
            self.dma(self.SCR[0:B, 0:D], self.cc_thr_out.ap()[:, :])
            self.BAR()
            vload(VD2B)
            self.BAR()
            self.TT(self.SCR[0:B, 0:D], self.SCR[0:B, 0:D], vrow, ALU.add)
            self.BAR()
            self.TT(th, th, self.SCR[0:B, 0:D], ALU.add)
            self.BAR()
            self.dbg(f"thstep{step}", th, [B, D])

        trows(th, D)
        for g4 in range(4):
            self.dma(self.WS[:, g4 * 1024:g4 * 1024 + 1024]
                     .rearrange("p (c n) -> p c n", c=4),
                     e["bcs"].ap()[g4 * 512:(g4 + 1) * 512, :]
                     .rearrange("(c p) n -> p c n", p=128), eng="gpsimd")
        self.BAR()
        for dc in range(DC):
            self.MM(self.PS_A[0:B, 0:256], self.SMALLB[:, dc * 4:(dc + 1) * 4],
                    self.WS[:, dc * 256:(dc + 1) * 256], dc == 0, dc == DC - 1)
        self.BAR()
        self.COPY(self.SCR[0:B, 0:256], self.PS_A[0:B, 0:256])
        self.BAR()
        self.dma(self.cc_ag_in.ap()[0:1, :], self.SCR[0:B, 0:256])
        self.BAR()
        self.CCALL("AllGather", ALU.bypass, self.cc_ag_in.ap()[:, :].opt(),
                   self.cc_ag_out.ap()[:, :].opt())
        for r in range(8):
            self.dma(tn[:, r * 256:(r + 1) * 256], self.cc_ag_out.ap()[r:r + 1, :])
        self.BAR()
        vload(VBCB)
        self.BAR()
        self.TT(tn, tn, vrow, ALU.add)
        self.BAR()
        self.dma(self.cc_th_in.ap()[:, :], tn)
        self.BAR()
        self.COPY(self.IDXU[0:4, 8:9], self.METAB[0:4, 4:5])
        self.BAR()
        self.IGATHER(self.SCR[0:4, 0:D], self.cc_th_in.ap()[:, :],
                     self.IDXU[0:4, 8:9], B - 1, tag="bc")
        self.BAR()
        for s in range(4):
            self.MM(self.PS_A[:, s * 512:(s + 1) * 512], self.ONESF[0:1, :],
                    self.SCR[0:1, s * 512:(s + 1) * 512], True, True)
        self.BAR()
        self.COPY(self.GAM[:, 2 * D:3 * D], self.PS_A[:, 0:D])
        self.BAR()

    def _attn(self):
        e = self.ein
        self.P.scope = "a_imp"
        W1B = self.XT2
        for dc in range(DC):
            self.dma(W1B[:, dc * 512:(dc + 1) * 512],
                     e["W1"].ap()[dc * 128:(dc + 1) * 128, :], eng="gpsimd")
        self.BAR()
        self.dma(self.PVAL[:, 0:4],
                 e["vecs"].ap()[VB1:VB1 + 1, 0:512].rearrange("one (m p) -> (one p) m", p=128), slow=True)
        self.BAR()
        for tcg in range(4):
            for m in range(4):
                ps = self.PS_A[:, m * 512:(m + 1) * 512]
                for dc in range(DC):
                    self.MM(ps, W1B[:, dc * 512 + m * 128:dc * 512 + (m + 1) * 128],
                            self.XT[:, dc * D + tcg * 512:dc * D + (tcg + 1) * 512],
                            dc == 0, dc == DC - 1)
            self.BAR()
            for m in range(4):
                self.ACT(self.HB[:, m * HALF + tcg * 512:m * HALF + (tcg + 1) * 512],
                         self.PS_A[:, m * 512:(m + 1) * 512], AF.Gelu,
                         bias=self.PVAL[:, m:m + 1])
            self.BAR()
        self.dma(self.PVAL[:, 4:8],
                 e["rows"].ap()[0:1, 6216:6216 + 512].rearrange("one (m p) -> (one p) m", p=128), slow=True)
        self.BAR()
        self.COPY(self.SMALLB[:, 0:4], self.PVAL[:, 4:8])
        self.BAR()
        for tcg in range(4):
            for m in range(4):
                self.MM(self.PS_B[0:1, 0:512], self.SMALLB[:, m:m + 1],
                        self.HB[:, m * HALF + tcg * 512:m * HALF + (tcg + 1) * 512],
                        m == 0, m == 3)
            self.BAR()
            self.COPY(self.SCR[0:1, D - 2048 + tcg * 512:D - 2048 + (tcg + 1) * 512], self.PS_B[0:1, 0:512])
            self.BAR()
        self.TS(self.SCR[0:1, 0:D], self.SCR[0:1, 0:D], self.METAB[0:1, 5:6], ALU.add)
        self.BAR()
        self.dbg("improw", self.SCR[0:1, 0:D], [1, HALF])

        if getattr(self, "attn_stop", 99) <= 1:
            return
        self.P.scope = "a_exch"
        self.dma(self.cc_sm_in.ap()[0:1, 0:HALF], self.SCR[0:1, 0:D])
        self.BAR()
        self.CCALL("AllGather", ALU.bypass, self.cc_sm_in.ap()[:, :].opt(),
                   self.cc_sm_out.ap()[:, :].opt())
        self.TT(self.PVAL[:, 15:16], self.POSALL[:, 0:1], self.METAB[:, 6:7], ALU.add)
        self.BAR()
        self.COPY(self.IDXU[0:2, 8:9], self.PVAL[0:2, 15:16])
        self.BAR()
        self.IGATHER(self.SCR[0:2, 0:HALF], self.cc_sm_out.ap()[:, :],
                     self.IDXU[0:2, 8:9], 7, tag="imp2")
        self.BAR()
        for hh in range(2):
            self.dma(self.s_impfull.ap()[hh * HALF:(hh + 1) * HALF, :],
                     self.SCR[hh:hh + 1, 0:HALF])
        self.BAR()
        self.dma(self.IMPS[:, 0:32],
                 self.s_impfull.ap()[:, 0:1].rearrange("(c p) one -> p (c one)", p=128), slow=True)
        self.BAR()
        self.dbg("impfull", self.IMPS[:, 0:32], [128, 32])

        self.P.scope = "a_qp"
        self.dma(self.SCR[0:1, 0:D], e["rows"].ap()[0:1, 0:D])
        self.BAR()
        for s in range(4):
            self.MM(self.PS_A[:, s * 512:(s + 1) * 512], self.ONESF[0:1, :],
                    self.SCR[0:1, s * 512:(s + 1) * 512], True, True)
        self.BAR()
        self.COPY(self.QP[:, :], self.PS_A[:, 0:D])
        self.BAR()

        if getattr(self, "attn_stop", 99) <= 2:
            return
        self.P.scope = "a_search"
        S = self.SMALL
        lo, hi, mid, cnt, pred, t0 = (S[0:1, 10:11], S[0:1, 11:12], S[0:1, 12:13],
                                      S[0:1, 13:14], S[0:1, 14:15], S[0:1, 15:16])
        midb = S[:, 16:17]
        self.RED(S[:, 18:19], self.IMPS[:, 0:32], ALU.max)
        rowmin = S[:, 19:20]
        self.P.emit("vector", lambda E: E.tensor_reduce(
            out=rowmin, in_=self.IMPS[:, 0:32], axis=AX.X, op=ALU.min))
        self.BAR()
        self.TRANS(self.PS_TF[0:1, 0:128], S[:, 18:19], self.IDENT)
        self.TRANS(self.PS_TF[0:1, 128:256], rowmin, self.IDENT)
        self.BAR()
        self.RED(hi, self.PS_TF[0:1, 0:128], ALU.max)
        self.RED(lo, self.PS_TF[0:1, 128:256], ALU.min)
        self.BAR()
        self.TS(lo, lo, -0.001, ALU.add)
        self.BAR()
        onecol128 = S[:, 21:22]
        self.MEMSET(onecol128, 1.0)
        self.BAR()
        for it in range(NSEARCH):
            self.TT(S[0:1, 17:18], lo, hi, ALU.add)
            self.BAR()
            self.MM(self.PS_B[:, 256:257], self.HALFF[0:1, :], S[0:1, 17:18],
                    True, True)
            self.BAR()
            cmp_ = S[:, 22:54]
            rc = S[:, 54:55]
            self.P.emit("vector", lambda E, _c=cmp_, _r=rc: E.tensor_scalar(
                out=_c, in0=self.IMPS[:, 0:32], scalar1=self.PS_B[:, 256:257],
                scalar2=0.0, op0=ALU.is_gt, op1=ALU.add, accum_out=_r))
            self.COPY(mid, self.PS_B[0:1, 256:257])
            self.BAR()
            self.MM(self.PS_B[0:1, 257:258], onecol128, rc, True, True)
            self.BAR()
            self.TS(pred, self.PS_B[0:1, 257:258], 511.5, ALU.is_gt)
            self.BAR()
            self.TT(t0, mid, lo, ALU.subtract)
            self.TT(S[0:1, 18:19], hi, mid, ALU.subtract)
            self.BAR()
            self.TT(t0, t0, pred, ALU.mult)
            self.TT(S[0:1, 18:19], S[0:1, 18:19], pred, ALU.mult)
            self.BAR()
            self.TT(lo, lo, t0, ALU.add)
            self.TT(hi, mid, S[0:1, 18:19], ALU.add)
            self.BAR()
        self.MM(self.PS_B[:, 256:257], self.ONESF[0:1, :], lo, True, True)
        self.BAR()
        lob = S[:, 16:17]
        self.COPY(lob, self.PS_B[:, 256:257])
        self.BAR()
        gt = S[:, 22:54]
        self.TS(gt, self.IMPS[:, 0:32], lob, ALU.is_gt)
        self.BAR()
        m1 = self.SCR[:, 0:32]
        m2 = self.SCR[:, 160:192]
        self.TS(m2, gt, -1.0e9, ALU.mult, 1.0e9, ALU.add)
        self.TT(m1, self.IMPS[:, 0:32], gt, ALU.mult)
        self.BAR()
        self.TT(m1, m1, m2, ALU.add)
        rmin = S[:, 54:55]
        self.BAR()
        self.P.emit("vector", lambda E: E.tensor_reduce(
            out=rmin, in_=m1, axis=AX.X, op=ALU.min))
        self.BAR()
        self.COPY(self.STAT[:, 60:61], rmin)
        self.COPY(self.STAT[:, 62:63], lob)
        self.TRANS(self.PS_TF[0:1, 0:128], rmin, self.IDENT)
        self.BAR()
        self.COPY(self.STAT[0:1, 61:62], self.PS_TF[0:1, 0:1])
        tau = S[0:1, 10:11]
        self.P.emit("vector", lambda E: E.tensor_reduce(
            out=tau, in_=self.PS_TF[0:1, 0:128], axis=AX.X, op=ALU.min))
        self.BAR()
        self.COPY(self.STAT[0:1, 63:64], tau)
        self.BAR()
        self.dbg("snap", self.STAT[:, 56:64], [128, 8])
        taub = S[:, 16:17]
        self.MM(self.PS_B[:, 256:257], self.ONESF[0:1, :], tau, True, True)
        self.BAR()
        self.COPY(taub, self.PS_B[:, 256:257])
        self.BAR()
        gt01 = self.SCR[:, 32:64]
        self.TS(gt01, self.IMPS[:, 0:32], taub, ALU.is_gt)
        eq01 = self.SCR[:, 64:96]
        self.TS(eq01, self.IMPS[:, 0:32], taub, ALU.is_equal)
        rc = S[:, 54:55]
        self.BAR()
        self.RED(rc, gt01, ALU.add)
        self.BAR()
        self.MM(self.PS_B[0:1, 257:258], onecol128, rc, True, True)
        self.BAR()
        ngt = S[0:1, 11:12]
        self.COPY(ngt, self.PS_B[0:1, 257:258])
        self.BAR()
        need = S[0:1, 12:13]
        self.TS(need, ngt, -512.0, ALU.add, -1.0, ALU.mult)
        self.BAR()
        needb = S[:, 17:18]
        self.MM(self.PS_B[:, 256:257], self.ONESF[0:1, :], need, True, True)
        self.BAR()
        self.COPY(needb, self.PS_B[:, 256:257])
        self.BAR()
        if getattr(self, "attn_stop", 99) <= 22:
            return
        sel = self.SCR[:, 96:128]
        self._cumsum_inc(eq01, self.SCR[:, 128:160])
        self.BAR()
        le = self.SCR[:, 160:192]
        self.TS(le, self.SCR[:, 128:160], needb, ALU.is_le)
        self.BAR()
        self.TT(le, le, eq01, ALU.mult)
        self.BAR()
        self.TT(sel, gt01, le, ALU.add)
        self.BAR()
        self.dbg("sel", sel, [128, 32])
        CS = self.SCR[:, 128:160]
        self._cumsum_inc(sel, CS)
        self.BAR()
        slotf = self.SCR[:, 160:192]
        self.TS(slotf, CS, -1.0, ALU.add)
        self.BAR()
        self.TT(slotf, slotf, sel, ALU.mult)
        ob = self.SCR[:, 192:224]
        self.TS(ob, sel, -1.0e9, ALU.mult, 1.0e9, ALU.add)
        self.BAR()
        self.TT(slotf, slotf, ob, ALU.add)
        self.BAR()
        slotu = self.IINT.bitcast(U32)[:, 0:32]
        self.COPY(slotu, slotf)
        self.BAR()
        for c in range(32):
            self.ISCATTER(self.s_idx.ap()[:, :], slotu[:, c:c + 1],
                          self.POSU[:, c:c + 1], KSEL - 1)
        self.BAR()
        self.dma(self.IDXU[:, 0:4],
                 self.s_idx.ap()[0:1, :].rearrange("one (k p) -> (one p) k", p=128), slow=True)
        self.BAR()
        self.COPY(self.PVAL[:, 0:4], self.IDXU[:, 0:4])
        self.BAR()
        self.dbg("posval", self.PVAL[:, 0:4], [128, 4])
        self.dbg("idxu", self.IDXU[:, 0:4], [128, 4], U32)
        minp = S[0:1, 10:11]
        self.COPY(minp, self.PVAL[0:1, 0:1])
        self.BAR()
        self.MM(self.PS_B[:, 256:257], self.ONESF[0:1, :], minp, True, True)
        self.BAR()
        minpb = S[:, 16:17]
        self.COPY(minpb, self.PS_B[:, 256:257])
        self.BAR()
        degrow = self.SCR[:, 0:HALF]
        self.TS(degrow, self.QP[:, :], minpb, ALU.is_ge)
        self.BAR()
        self.TT(self.COSFB[:, :], self.COSFB[:, :], degrow, ALU.mult)
        self.TT(self.SINFB[:, :], self.SINFB[:, :], degrow, ALU.mult)
        self.BAR()
        for kc in range(4):
            for tcg in range(4):
                kk = self.KEEP[:, (kc * 4 + tcg) * 512:(kc * 4 + tcg + 1) * 512]
                qslab = self.QP[:, tcg * 512:(tcg + 1) * 512]
                self.TS(self.SCR[:, 0:512], qslab, self.PVAL[:, kc:kc + 1], ALU.is_ge)
                self.TS(self.SCR[:, 512:1024], qslab, minpb, ALU.is_lt)
                self.BAR()
                self.TT(kk, self.SCR[:, 0:512], self.SCR[:, 512:1024], ALU.max)
                self.BAR()
        if getattr(self, "attn_stop", 99) <= 3:
            return
        self.P.scope = "a_gather"
        idl = self.PVAL[:, 4:8]
        self.TS(idl, self.PVAL[:, 0:4], self.METAB[:, 0:1], ALU.subtract)
        self.BAR()
        neg = self.PVAL[:, 8:12]
        self.TS(neg, idl, 0.0, ALU.is_lt)
        self.BAR()
        self.STT(idl, neg, 1.0e9, ALU.mult, idl, ALU.add)
        self.BAR()
        self.COPY(self.IDXU[:, 4:8], idl)
        self.BAR()
        XS = self.XT2
        for kc in range(4):
            self.MEMSET(XS[:, kc * D:(kc + 1) * D], 0.0)
        self.BAR()
        for kc in range(4):
            self.IGATHER(XS[:, kc * D:(kc + 1) * D], self.s_x1.ap()[:, :],
                         self.IDXU[:, 4 + kc:5 + kc], HALF - 1, tag="xsel")
        for kc in range(4):
            self.IGATHER(self.PVAL[:, 8 + kc:9 + kc], self.s_impfull.ap()[:, :],
                         self.IDXU[:, kc:kc + 1], L - 1, tag="gate")
        self.BAR()
        gcol = self.PVAL[:, 12:16]
        self.ACT(gcol, self.PVAL[:, 8:12], AF.Sigmoid)
        self.BAR()
        if getattr(self, "attn_stop", 99) <= 4:
            return
        self.P.scope = "a_agx"
        for kc in range(4):
            self.dma(self.cc_xs_in.ap()[kc * 128:(kc + 1) * 128, :],
                     XS[:, kc * D:(kc + 1) * D])
        self.BAR()
        self.CCALL("AllGather", ALU.bypass, self.cc_xs_in.ap()[:, :].opt(),
                   self.cc_xs_out.ap()[:, :].opt())
        pbase = self.PVAL[:, 8:9]
        self.COPY(pbase, self.POSALL[:, 0:1])
        self.BAR()
        for kc in range(4):
            self.TS(self.PVAL[:, 9:10], self.METAB[:, 1:2], 512.0, ALU.mult,
                    float(kc * 128), ALU.add)
            self.BAR()
            self.TT(self.PVAL[:, 9:10], self.PVAL[:, 9:10], pbase, ALU.add)
            self.BAR()
            self.COPY(self.IDXU[:, 8:9], self.PVAL[:, 9:10])
            self.BAR()
            self.IGATHER(self.WS[:, 0:D], self.cc_xs_out.ap()[:, :],
                         self.IDXU[:, 8:9], 8 * KSEL - 1, tag="merge")
            self.BAR()
            self.TT(XS[:, kc * D:(kc + 1) * D], XS[:, kc * D:(kc + 1) * D],
                    self.WS[:, 0:D], ALU.add)
            self.BAR()
        self.dbg("xsel", XS[:, 0:D], [128, D], BF16)

        if getattr(self, "attn_stop", 99) <= 5:
            return
        KTVB = self.POOLA.bitcast(BF16)
        KT0 = 1024
        VB0 = 3072
        EK0 = 5152
        RB0 = 7200
        for kc in range(4):
            ang = self.SCR[:, 0:64]
            self.TS(ang, self.INVB[:, :], self.PVAL[:, kc:kc + 1], ALU.mult)
            self.BAR()
            self._sincos(ang, 128, 64, self.SCR[:, 1024 + 64:1024 + 128],
                         self.SCR[:, 1024:1024 + 64], scr=self.SCR[:, 256:1024])
            self.BAR()
            self.TRANS(self.PS_TF[:, kc * 128:(kc + 1) * 128],
                       self.SCR[:, 1024:1024 + 128], self.IDENT)
            self.BAR()
        self.COPY(self.POOLA[:, 0:512], self.PS_TF[:, 0:512])
        self.BAR()
        C2S = self.XH[:, 0:512]
        S2S = self.XH[:, 512:1024]
        self.COPY(C2S[0:64, :], self.POOLA[0:64, 0:512])
        self.COPY(S2S[64:128, :], self.POOLA[64:128, 0:512])
        self.BAR()
        self.dma(C2S[64:128, :], C2S[0:64, :])
        self.dma(S2S[0:64, :], S2S[64:128, :])
        self.BAR()
        self.TS(S2S[0:64, :], S2S[0:64, :], -1.0, ALU.mult)
        self.BAR()
        self.P.scope = "a_kv"
        XST = self.XT2[:, 8192:16384]
        xst3 = XST.rearrange("p (dc n) -> p dc n", dc=16)
        for kc in range(4):
            for g in range(2):
                for j in range(8):
                    dc = g * 8 + j
                    self.TRANS(self.PS_T[:, j * 128:(j + 1) * 128],
                               XS[:, kc * D + dc * 128:kc * D + (dc + 1) * 128])
                self.BAR()
                self.COPY(xst3[:, g * 8:(g + 1) * 8, kc * 128:(kc + 1) * 128],
                          self.PS_T[:, 0:1024].rearrange("p (j n) -> p j n", j=8))
                self.BAR()
        ktok = {}

        def ldkv(m):
            half = m % 2
            ktok[m] = self.dma(self.WS[:, half * 2048:(half + 1) * 2048]
                               .rearrange("p (c n) -> p c n", c=16),
                               e["Wkv"].ap()[:, m * 128:(m + 1) * 128]
                               .rearrange("(c p) n -> p c n", p=128),
                               eng="gpsimd", nb=True)
        ldkv(0)
        self.BAR(fence4=ktok[0])
        for mq in range(2):
            for m4 in range(4):
                m = mq * 4 + m4
                if m + 1 < 8:
                    ldkv(m + 1)
                wsb = self.WS[:, (m % 2) * 2048:(m % 2 + 1) * 2048]
                for dc in range(DC):
                    self.MM(self.PS_A[:, m4 * 512:(m4 + 1) * 512],
                            wsb[:, dc * 128:(dc + 1) * 128], XST[:, dc * 512:(dc + 1) * 512],
                            dc == 0, dc == DC - 1)
                self.BAR(fence4=ktok.get(m + 1))
            dstkv = self.SCR[:, 0:D] if mq == 0 else self.WSF[:, 0:D]
            self.COPY(dstkv, self.PS_A[:, 0:2048])
            self.BAR()
        self.P.scope = "a_krope"
        for kvh in range(KVH):
            kk = self.SCR[:, kvh * 512:(kvh + 1) * 512]
            ksw = self.SMALL[:, 0:512]
            self.dma(ksw[0:64, :], kk[64:128, :])
            self.dma(ksw[64:128, :], kk[0:64, :])
            self.BAR()
            self.TT(ksw, ksw, S2S, ALU.mult)
            self.TT(kk, kk, C2S, ALU.mult)
            self.BAR()
            self.TT(KTVB[:, KT0 + kvh * 512:KT0 + (kvh + 1) * 512], kk, ksw, ALU.add)
            self.BAR()
        self.dbg("krope", KTVB[:, KT0:KT0 + 2048], [128, 2048], BF16)
        self.P.scope = "a_v"
        for kvh in range(KVH):
            for kc in range(4):
                self.TRANS(self.PS_TF[:, kc * 128:(kc + 1) * 128],
                           self.WSF[:, kvh * 512 + kc * 128:kvh * 512 + (kc + 1) * 128],
                           self.IDENT)
            self.BAR()
            for kc in range(4):
                vcol = VB0 + (kvh * 4 + kc) * 130
                self.TS(KTVB[:, vcol:vcol + 128], self.PS_TF[:, kc * 128:(kc + 1) * 128],
                        gcol[:, kc:kc + 1], ALU.mult)
                self.MEMSET(KTVB[:, vcol + 128:vcol + 129], 1.0)
            self.BAR()

        if getattr(self, "attn_stop", 99) <= 6:
            return
        self.P.scope = "a_q"
        QT = self.XT2
        WSFB = self.WSF.bitcast(BF16)

        qtok = {}

        def ldq(h):
            half = h % 2
            qtok[h] = self.dma(self.WS[:, half * 2048:(half + 1) * 2048]
                               .rearrange("p (c n) -> p c n", c=16),
                               e["Wq"].ap()[:, h * 128:(h + 1) * 128]
                               .rearrange("(c p) n -> p c n", p=128),
                               eng="gpsimd", nb=True)
        ldq(0)
        self.BAR(fence4=qtok[0])
        for h in range(H):
            if h + 1 < H:
                ldq(h + 1)
            wsb = self.WS[:, (h % 2) * 2048:(h % 2 + 1) * 2048]
            for tcg in range(4):
                ps = self.PS_A[:, tcg * 512:(tcg + 1) * 512]
                for dc in range(DC):
                    self.MM(ps, wsb[:, dc * 128:(dc + 1) * 128],
                            self.XT[:, dc * D + tcg * 512:dc * D + (tcg + 1) * 512],
                            dc == 0, dc == DC - 1)
            self.BAR()
            qz = self.XH[:, 0:D]
            self.COPY(qz, self.PS_A[:, 0:2048])
            self.BAR()
            qsw = WSFB[:, 0:D]
            self.dma(qsw[0:64, :], qz[64:128, :])
            self.dma(qsw[64:128, :], qz[0:64, :])
            self.BAR()
            t1 = WSFB[:, D:2 * D]
            self.TT(t1, qz, self.COSFB[:, :], ALU.mult)
            self.TT(qsw, qsw, self.SINFB[:, :], ALU.mult)
            self.BAR()
            self.TT(QT[:, h * D:(h + 1) * D], t1, qsw, ALU.add)
            self.BAR(fence4=qtok.get(h + 1))
        self.dbg("qrope", QT[:, 0:D], [128, D], BF16)

        if getattr(self, "attn_stop", 99) <= 7:
            return
        self.P.scope = "a_scores"
        AOT = self.XT
        NT = H * 4

        def op_s(t):
            h, tcg, kvh = t // 4, t % 4, t // 16
            for kc in range(4):
                self.MM(self.PS_A[:, kc * 512:(kc + 1) * 512],
                        KTVB[:, KT0 + kvh * 512 + kc * 128:KT0 + kvh * 512 + (kc + 1) * 128],
                        QT[:, h * D + tcg * 512:h * D + (tcg + 1) * 512],
                        True, True)

        def op_e(t):
            for kc in range(4):
                self.ACT(KTVB[:, EK0 + kc * 512:EK0 + (kc + 1) * 512],
                         self.PS_A[:, kc * 512:(kc + 1) * 512], AF.Exp,
                         scale=SCALE, bias=-SHIFT)

        def op_k(t):
            tcg = t % 4
            for kc in range(4):
                ekk = KTVB[:, EK0 + kc * 512:EK0 + (kc + 1) * 512]
                self.TT(ekk, ekk,
                        self.KEEP[:, (kc * 4 + tcg) * 512:(kc * 4 + tcg + 1) * 512],
                        ALU.mult)

        def op_p(t):
            kvh = t // 16
            for kc in range(4):
                vcol = VB0 + (kvh * 4 + kc) * 130
                self.MM(self.PS_B[:, 0:512], KTVB[:, vcol:vcol + 128],
                        KTVB[:, EK0 + kc * 512:EK0 + (kc + 1) * 512], kc == 0, kc == 3)
            for kc in range(4):
                vcol = VB0 + (kvh * 4 + kc) * 130
                self.MM(self.PS_B[0:1, 512:1024], KTVB[:, vcol + 128:vcol + 129],
                        KTVB[:, EK0 + kc * 512:EK0 + (kc + 1) * 512], kc == 0, kc == 3)

        def op_r(t):
            self.P.emit("vector", lambda E, _d=self.PS_B[0:1, 512:1024]:
                        E.reciprocal(out=self.SCR[0:1, 0:512], in_=_d))

        def op_b(t):
            self.MM(self.PS_TF[:, 0:512], self.ONESF[0:1, :],
                    self.SCR[0:1, 0:512], True, True)

        def op_v(t):
            self.ACT(self.SMALL[:, 0:512], self.PS_B[:, 0:512], AF.Identity)

        def op_a(t):
            h, tcg = t // 4, t % 4
            self.TT(AOT[:, h * D + tcg * 512:h * D + (tcg + 1) * 512],
                    self.SMALL[:, 0:512], self.PS_TF[:, 0:512], ALU.mult)

        for step in range(-2, NT + 1):
            if 0 <= step < NT:
                op_p(step)
            if 0 <= step - 1 < NT:
                op_a(step - 1)
            self.BAR()
            if 0 <= step + 1 < NT:
                op_e(step + 1)
            if 0 <= step < NT:
                op_r(step)
            self.BAR()
            if 0 <= step < NT:
                op_b(step)
                op_v(step)
            if 0 <= step + 2 < NT:
                op_s(step + 2)
            if 0 <= step + 1 < NT:
                op_k(step + 1)
            self.BAR()
        self.dbg("aot", AOT[:, 0:D], [128, D], BF16)

        if getattr(self, "attn_stop", 99) <= 8:
            return
        self.P.scope = "a_wo"
        WSFB2 = self.WSF.bitcast(BF16)
        wtokh = {}
        stokh = {}

        def ldwo(slab):
            buf = self.WS if slab % 2 == 0 else WSFB2
            wtokh[slab] = self.dma(
                buf[:, 0:4096].rearrange("p (c n) -> p c n", c=16),
                e["Wo"].ap()[:, slab * 256:(slab + 1) * 256]
                .rearrange("(c p) n -> p c n", p=128), eng="gpsimd", nb=True)
        ldwo(0)
        self.BAR(fence4=wtokh[0])
        for slab in range(8):
            buf = self.WS if slab % 2 == 0 else WSFB2
            for tc in range(TC):
                ps = self.PS_A[:, (tc % 8) * 256:(tc % 8 + 1) * 256]
                for dc in range(DC):
                    self.MM(ps, AOT[:, dc * D + tc * 128:dc * D + (tc + 1) * 128],
                            buf[:, dc * 256:(dc + 1) * 256], dc == 0, dc == DC - 1)
                if tc % 8 == 7:
                    sk = stokh.get((slab - (1 if tc == 7 else 0),
                                    15 if tc == 7 else 7))
                    self.BAR(fence3=sk if NBWO else None)
                    for t2 in range(tc - 7, tc + 1):
                        self.COPY(self.SCR[:, (t2 % 8) * 256:(t2 % 8 + 1) * 256],
                                  self.PS_A[:, (t2 % 8) * 256:(t2 % 8 + 1) * 256])
                    self.BAR()
                    for t2 in range(tc - 7, tc + 1):
                        stokh[(slab, tc)] = self.dma(
                            self.s_wo.ap()[t2 * 128:(t2 + 1) * 128,
                                           slab * 256:(slab + 1) * 256],
                            self.SCR[:, (t2 % 8) * 256:(t2 % 8 + 1) * 256], nb=NBWO)
                    if tc == 7 and slab + 1 < 8:
                        ldwo(slab + 1)
                    if tc == 15:
                        self.BAR(fence4=wtokh.get(slab + 1))
                    else:
                        self.BAR()
        self.P.scope = "a_out2"
        tco = [15] + list(range(15))

        xt2f = self.XT2.bitcast(F32)

        def o2bufs(i):
            p = i % 3
            if p == 0:
                return self.SCR[:, 0:D], self.WSF[:, 0:D]
            if p == 1:
                return self.POOLA[:, 0:D], self.POOLA[:, D:2 * D]
            return xt2f[:, 0:D], xt2f[:, D:2 * D]

        otok = {}
        wtok = {}

        def ldo2(i):
            tc = tco[i]
            b1, b2 = o2bufs(i)
            self.dma(b1, self.s_wo.ap()[tc * 128:(tc + 1) * 128, :], nb=NBO2)
            otok[i] = self.dma(b2, self.s_out1.ap()[tc * 128:(tc + 1) * 128, :],
                               nb=NBO2)
        def _f(v):
            return v if NBO2 else None
        self.BAR(fence3=self.P.ds3)
        ldo2(0)
        self.BAR(fence3=_f(otok[0]))
        for i, tc in enumerate(tco):
            b1, b2 = o2bufs(i)
            if i + 1 < TC:
                ldo2(i + 1)
            self.TT(b1, b1, b2, ALU.add)
            self.BAR()
            wtok[i] = self.dma(self.s_out1.ap()[tc * 128:(tc + 1) * 128, :], b1,
                               nb=NBO2)
            cands = [x for x in (otok.get(i + 1), wtok.get(i - 1)) if x is not None]
            self.BAR(fence3=_f(max(cands)) if cands else None)
            if tc == 0:
                self.dbg("out2t0", b1, [128, D])
            if tc == 15:
                self.dma(self.cc_sm_in.ap()[0:1, 0:D], self.SCR[127:128, 0:D])
                self.BAR()
                self.CCALL("AllGather", ALU.bypass, self.cc_sm_in.ap()[:, :].opt(),
                           self.cc_sm_out.ap()[:, :].opt())
                for bb in range(B):
                    self.dma(self.POOLB[bb:bb + 1, D:2 * D],
                             self.cc_sm_out.ap()[2 * bb + 1:2 * bb + 2, 0:D])
                self.BAR()
        self.BAR(fence3=self.P.ds3, fence4=self.P.ds4)

    def _cumsum_inc(self, src01, dst):
        self.MM(self.PS_B[:, 0:32], self.UT[:, :], src01, True, True)
        self.BAR()
        pre = self.SCR[:, 224:256]
        self.COPY(pre, self.PS_B[:, 0:32])
        self.BAR()
        self.dma(self.SCR[0:1, 320:352], pre[127:128, 0:32])
        self.BAR()
        self.TRANS(self.PS_TF[0:32, 32:33], self.SCR[0:1, 320:352], self.IDENT[0:1, 0:1])
        self.BAR()
        colsum = self.SCR[0:32, 256:257]
        self.COPY(colsum, self.PS_TF[0:32, 32:33])
        self.BAR()
        self.MM(self.PS_B[0:32, 33:34], self.SUT32[:, :], colsum, True, True)
        self.BAR()
        offs = self.SCR[0:32, 257:258]
        self.COPY(offs, self.PS_B[0:32, 33:34])
        self.BAR()
        self.TRANS(self.PS_TF[0:1, 34:66], offs, self.IDENT[0:32, 0:32])
        self.BAR()
        self.COPY(self.SCR[0:1, 258:290], self.PS_TF[0:1, 34:66])
        self.BAR()
        self.MM(self.PS_B[:, 64:96], self.ONESF[0:1, :], self.SCR[0:1, 258:290],
                True, True)
        self.BAR()
        offsb = self.SCR[:, 290:322]
        self.COPY(offsb, self.PS_B[:, 64:96])
        self.BAR()
        self.TT(dst, pre, offsb, ALU.add)
        self.BAR()


def _mk_cst():
    cst = np.zeros((128, 512), np.float32)
    cst[:, 0:128] = np.eye(128, dtype=np.float32)
    i = np.arange(128)
    cst[:, 128:256] = (i[:, None] <= i[None, :]).astype(np.float32)
    j = np.arange(32)
    cst[0:32, 320:352] = (j[:, None] < j[None, :]).astype(np.float32)
    return cst


_NC_CACHE = {}


def _get_nc(debug=(), phases=("ph0", "ffn1", "attn", "ffn3")):
    key = (tuple(sorted(debug)), tuple(sorted(phases)))
    if key not in _NC_CACHE:
        kb = KB(debug, phases)
        nc = kb.build()
        _NC_CACHE[key] = (nc, kb.dbg_names)
    return _NC_CACHE[key]


def make_in_maps(inputs):
    f32 = np.float32
    x = np.ascontiguousarray(inputs["L0_attn_x"], f32)
    vecs = np.zeros((16, D), f32)
    vecs[VB1, 0:512] = inputs["L0_attn_b1"]
    vecs[VGATEB] = inputs["L0_ffn_gate_b"]
    vecs[VAGGB] = inputs["L0_ffn_agg_b"]
    vecs[VBCB] = inputs["L0_ffn_bc_b"]
    vecs[VD1B, 0:DHS] = 0
    vecs[VD2B] = inputs["L0_ffn_d2_b"]
    vecs[VING] = inputs["L0_ffn_in_gamma"]
    vecs[VINB] = inputs["L0_ffn_in_beta"]
    vecs[VPNG] = inputs["L0_ffn_pn_gamma"]
    vecs[VPNB] = inputs["L0_ffn_pn_beta"]
    vecs[VOUTG] = inputs["L0_ffn_out_gamma"]
    vecs[VOUTB] = inputs["L0_ffn_out_beta"]
    cst = _mk_cst()
    inv = (1.0 / (f32(10000.0) ** (np.arange(0, 128, 2, dtype=f32) / f32(128)))).astype(f32)
    in_maps = []
    for c in range(8):
        b, hf = c // 2, c % 2
        goff = hf * HALF
        im = {}
        im["x_own"] = np.ascontiguousarray(x[b, goff:goff + HALF, :])
        im["x_last"] = np.ascontiguousarray(x[:, L - 1, :])
        im["Wq"] = inputs["L0_attn_Wq"]
        im["Wkv"] = inputs["L0_attn_Wkv"]
        im["Wo"] = inputs["L0_attn_Wo"]
        im["W1"] = inputs["L0_attn_W1"]
        im["gate_w"] = inputs["L0_ffn_gate_w"]
        im["d1s"] = np.ascontiguousarray(inputs["L0_ffn_d1_w"][:, c * DHS:(c + 1) * DHS])
        im["d2s"] = np.ascontiguousarray(inputs["L0_ffn_d2_w"][c * DHS:(c + 1) * DHS, :])
        im["aggs"] = np.ascontiguousarray(inputs["L0_ffn_agg_w"][:, c * 256:(c + 1) * 256])
        im["bcs"] = np.ascontiguousarray(inputs["L0_ffn_bc_w"][:, c * 256:(c + 1) * 256])
        v = vecs.copy()
        v[VD1B, 0:DHS] = inputs["L0_ffn_d1_b"][c * DHS:(c + 1) * DHS]
        im["vecs"] = v
        cstc = cst.copy()
        pos = (np.arange(4096) % 128 * 0 + np.arange(4096))
        posall = np.arange(L).reshape(32, 128).T.astype(f32)
        cstc[:, 256:288] = posall
        im["cst"] = cstc
        rows = np.zeros((1, 8192), f32)
        rows[0, 0:HALF] = np.arange(goff, goff + HALF, dtype=f32)
        rows[0, 2048:2048 + L] = np.arange(L, dtype=f32)
        rows[0, 6144:6208] = inv
        rows[0, 6208:6216] = [goff, c ^ 1, c, 0, b, float(inputs["L0_attn_b2"][0]),
                              2 * b, 2 * b + 1]
        rows[0, 6216:6216 + 512] = np.asarray(inputs["L0_attn_W2"], f32).ravel()
        im["rows"] = rows
        in_maps.append({k: np.ascontiguousarray(np.asarray(v2, f32)) for k, v2 in im.items()})
    return in_maps


def kernel(**inputs):
    nc, _ = _get_nc()
    in_maps = make_in_maps(inputs)
    res = run_bass_kernel_spmd(nc, in_maps, core_ids=list(range(8)))
    out = np.zeros((B, L, D), np.float32)
    for c in range(8):
        b, hf = c // 2, c % 2
        out[b, hf * HALF:(hf + 1) * HALF, :] = res.results[c]["out_own"]
    return out
